# version 04
# baseline (speedup 1.0000x reference)
import numpy as np
import concourse.bass as bass
import concourse.mybir as mybir
from concourse.bass_utils import run_bass_kernel_spmd

F32 = mybir.dt.float32
BF16 = mybir.dt.bfloat16
AF = mybir.ActivationFunctionType
ALU = mybir.AluOpType
AX = mybir.AxisListType

N = 4096
D = 2048
HD = 128
NHC = 4
MINE = 1024
HID = 8192
NDC = D // 128
NHT = HID // 128
NKT = N // 128
NTT = MINE // 128
ISCALE = float(1.0 / np.sqrt(HD))
EPS = 1e-5
GROUPS = [[0, 1, 2, 3], [4, 5, 6, 7]]


def build():
    nc = bass.Bass("TRN2", target_bir_lowering=False)

    xt = nc.dram_tensor("xt", [D, N], F32, kind="ExternalInput").ap()
    xres = nc.dram_tensor("xres", [MINE, D], F32, kind="ExternalInput").ap()
    wq = nc.dram_tensor("wq", [128, NDC, NHC, HD], F32, kind="ExternalInput").ap()
    wo = nc.dram_tensor("wo", [D, D], F32, kind="ExternalInput").ap()
    bo = nc.dram_tensor("bo", [D], F32, kind="ExternalInput").ap()
    wg = nc.dram_tensor("wg", [NHT, 128, NDC, 128], F32, kind="ExternalInput").ap()
    wu = nc.dram_tensor("wu", [NHT, 128, NDC, 128], F32, kind="ExternalInput").ap()
    wd = nc.dram_tensor("wd", [HID, D], F32, kind="ExternalInput").ap()
    eye = nc.dram_tensor("eye", [128, 128], F32, kind="ExternalInput").ap()
    out = nc.dram_tensor("out", [MINE, D], F32, kind="ExternalOutput").ap()

    ag_send = [nc.dram_tensor(f"ag_send{h}", [HD, N], BF16) for h in range(NHC)]
    ag_recv = [nc.dram_tensor(f"ag_recv{h}", [4 * HD, N], BF16) for h in range(NHC)]
    zdr = nc.dram_tensor("zdr", [NHC, N], F32)
    h1d = nc.dram_tensor("h1d", [MINE, D], F32)
    gtd = nc.dram_tensor("gtd", [HID, MINE], BF16)
    wdb = nc.dram_tensor("wdb", [HID, D], BF16)

    sem_names = [
        "s_pre", "s_xt0", "s_xt1", "s_xt2", "s_xt3",
        "s_qev", "s_peq", "s_vt", "s_vte",
        "s_sS", "s_exp", "s_zac", "s_av", "s_otev", "s_zrow", "s_zw", "s_zbc", "s_otA", "s_otn",
        "s_ag", "s_cc",
        "s_wo", "s_otg0", "s_otg1", "s_otg2", "s_otg3", "s_xr0", "s_xr1", "s_bo", "s_wope", "s_ypsfree",
        "s_var1", "s_actr", "s_h1f", "s_h1d0", "s_h1d1", "s_tp", "s_tpe",
        "s_wg0", "s_wg1", "s_wg2", "s_gpe", "s_upe", "s_silu", "s_gt",
        "s_gtw0", "s_gtw1", "s_gtw2",
        "s_wdr0", "s_wdr1", "s_wdr2", "s_wdr3",
        "s_wdw0", "s_wdw1", "s_wdw2", "s_wdw3",
        "s_gtr0", "s_gtr1", "s_wdd0", "s_wdd1", "s_wdd2", "s_dhc", "s_y2free",
        "s_var2", "s_actr2", "s_h1ld0", "s_h1ld1", "s_ovr", "s_out0", "s_out1",
    ]

    import contextlib
    with contextlib.ExitStack() as est:
        S = {n: est.enter_context(nc.semaphore(n)) for n in sem_names}
        ps = est.enter_context(nc.psum_tensor("ps", [128, 4096], F32))

        def bank(i, n=1):
            return ps[:, 512 * i: 512 * (i + n)]

        def bank_bf(i):
            return ps[:, 512 * i: 512 * i + 64].bitcast(BF16)

        def bank32(i):
            return ps[:, 512 * i: 512 * i + 128]

        eye_bf = est.enter_context(nc.sbuf_tensor("eye_bf", [128, 128], BF16))
        eye32 = est.enter_context(nc.sbuf_tensor("eye32", [128, 128], F32))
        big64 = est.enter_context(nc.sbuf_tensor("big64", [128, 2 * NHT * 256], BF16))
        wo_view = big64[:, :].rearrange("p (dc n) -> p dc n", dc=NDC)
        gtp_view = big64[:, :].rearrange("p (b hc m) -> p b hc m", b=2, hc=NHT)
        rs = est.enter_context(nc.sbuf_tensor("rs", [128, 1], F32))
        negmean = est.enter_context(nc.sbuf_tensor("negmean", [128, 1], F32))
        ssq = est.enter_context(nc.sbuf_tensor("ssq", [128, 1], F32))
        var_ = est.enter_context(nc.sbuf_tensor("var_", [128, 1], F32))
        lnv = est.enter_context(nc.sbuf_tensor("lnv", [128, 1], F32))
        rstd = est.enter_context(nc.sbuf_tensor("rstd", [128, 1], F32))
        epsv = est.enter_context(nc.sbuf_tensor("epsv", [128, 1], F32))
        wdstage = est.enter_context(nc.sbuf_tensor("wdstage", [128, 1, D], BF16))

        regs = {}

        with (
            nc.sbuf_tensor("qT", [128, NHC, N], BF16) as qT,
            nc.sbuf_tensor("v_sb", [128, NHC, NKT, HD], BF16) as v_sb,
            nc.sbuf_tensor("xt_sb", [128, 3, 1024], BF16) as xt_sb,
            nc.sbuf_tensor("wq_sb", [128, NDC, NHC, HD], BF16) as wq_sb,
            nc.sbuf_tensor("P_sb", [128, 2, 1024], BF16) as P_sb,
            nc.sbuf_tensor("ot_raw", [128, 2, N], F32) as ot_raw,
            nc.sbuf_tensor("zp", [128, NHC, 4, NKT], F32) as zp,
            nc.sbuf_tensor("zs", [128, NKT], F32) as zs,
            nc.sbuf_tensor("zinv_s", [128, NKT], F32) as zinv_s,
            nc.sbuf_tensor("zrowt", [32, 128], F32) as zrowt,
            nc.sbuf_tensor("zinvb", [128, N // 2], F32) as zinvb,
            nc.sbuf_tensor("otn", [128, N // 2], BF16) as otn,
            nc.Block() as blk,
        ):
            @blk.gpsimd
            def _(g):
                g.dma_start(out=wq_sb[:, :, :, :].rearrange("p dc h c -> p (dc h c)"),
                            in_=wq.rearrange("p dc h c -> p (dc h c)")).then_inc(S["s_pre"], 16)
                g.dma_start(out=eye_bf[:, :], in_=eye).then_inc(S["s_pre"], 16)
                for s4 in range(4):
                    for dc in range(NDC):
                        j = 16 * s4 + dc
                        if j >= 3:
                            g.wait_ge(S["s_peq"], j - 2)
                        g.dma_start(
                            out=xt_sb[:, j % 3, :],
                            in_=xt[dc * 128:(dc + 1) * 128, s4 * 1024:(s4 + 1) * 1024],
                        ).then_inc(S[f"s_xt{j % 3}"], 16)
                for dq in range(4):
                    g.dma_start(
                        out=wo_view[:, dq * 4:(dq + 1) * 4, :],
                        in_=wo[dq * 512:(dq + 1) * 512, :].rearrange("(dc p) n -> p dc n", p=128),
                    ).then_inc(S["s_wo"], 16)
                def wd_job(lo, hi):
                    for ht2 in range(lo, hi):
                        if ht2 >= 1:
                            g.wait_ge(S["s_wdw0"], 16 * ht2)
                        g.dma_start(
                            out=wdstage[:, 0, :],
                            in_=wd[ht2 * 128:(ht2 + 1) * 128, :],
                        ).then_inc(S["s_wdr0"], 16)
                        g.wait_ge(S["s_wdr0"], 16 * (ht2 + 1))
                        g.dma_start(
                            out=wdb.ap()[ht2 * 128:(ht2 + 1) * 128, :],
                            in_=wdstage[:, 0, :],
                        ).then_inc(S["s_wdw0"], 16)

                for h in range(NHC):
                    wd_job(16 * h, 16 * (h + 1))
                    g.wait_ge(S["s_otA"], h + 1)
                    g.dma_start(out=ag_send[h].ap()[:, 0:N // 2], in_=otn[:, :]).then_inc(S["s_ag"], 16)
                    g.wait_ge(S["s_otn"], h + 1)
                    g.dma_start(out=ag_send[h].ap()[:, N // 2:N], in_=otn[:, :]).then_inc(S["s_ag"], 16)
                    g.wait_ge(S["s_ag"], 32 * (h + 1))
                    g.collective_compute(
                        "AllGather", ALU.bypass, replica_groups=GROUPS,
                        ins=[ag_send[h].ap()], outs=[ag_recv[h].ap()],
                    ).then_inc(S["s_cc"], 1)


            @blk.tensor
            def _(t):
                t.wait_ge(S["s_pre"], 32)
                for s4 in range(4):
                    for dc in range(NDC):
                        j = 16 * s4 + dc
                        t.wait_ge(S[f"s_xt{j % 3}"], 16 * (j // 3 + 1))
                        if dc == 0 and s4 > 0:
                            t.wait_ge(S["s_qev"], 4 * s4)
                        ins = None
                        for h in range(NHC):
                            for half in range(2):
                                ins = t.matmul(
                                    bank(2 * h + half),
                                    wq_sb[:, dc, h, :],
                                    xt_sb[:, j % 3, 512 * half:512 * (half + 1)],
                                    start=(dc == 0), stop=(dc == NDC - 1),
                                )
                        ins.then_inc(S["s_peq"], 1)
                for h in range(NHC):
                    for kt in range(NKT):
                        j = NKT * h + kt
                        if j == 0:
                            t.wait_ge(S["s_qev"], 16)
                        if j >= 2:
                            t.wait_ge(S["s_vte"], j - 1)
                        t.transpose(
                            bank_bf(j % 2),
                            qT[:, h, kt * 128:(kt + 1) * 128],
                            eye_bf[:, :],
                        ).then_inc(S["s_vt"], 1)
                for gidx in range(16):
                    h, qq = gidx // 4, gidx % 4
                    po = gidx % 2

                    def emit_S(kt, gidx=gidx, h=h, qq=qq):
                        idx = 32 * gidx + kt
                        if idx == 0:
                            t.wait_ge(S["s_vte"], 128)
                        par = kt % 2
                        ins = None
                        for half in range(2):
                            ins = t.matmul(
                                bank(2 * par + half),
                                qT[:, h, kt * 128:(kt + 1) * 128],
                                qT[:, h, qq * 1024 + 512 * half: qq * 1024 + 512 * (half + 1)],
                                start=True, stop=True,
                            )
                        ins.then_inc(S["s_sS"], 1)

                    def emit_AV(kt, gidx=gidx, h=h, po=po):
                        idx = 32 * gidx + kt
                        t.wait_ge(S["s_zac"], idx + 1)
                        if kt == 0 and gidx >= 2:
                            t.wait_ge(S["s_otev"], gidx - 1)
                        ins = None
                        for half in range(2):
                            ins = t.matmul(
                                bank(4 + 2 * po + half),
                                v_sb[:, h, kt, :],
                                P_sb[:, kt % 2, 512 * half:512 * (half + 1)],
                                start=(kt == 0), stop=(kt == NKT - 1),
                            )
                        ins.then_inc(S["s_av"], 1)

                    emit_S(0)
                    emit_S(1)
                    for kt in range(NKT):
                        emit_AV(kt)
                        if kt + 2 < NKT:
                            emit_S(kt + 2)

            @blk.scalar
            def _(sc):
                for gidx in range(16):
                    h, qq = gidx // 4, gidx % 4
                    for kt in range(NKT):
                        idx = 32 * gidx + kt
                        par = kt % 2
                        sc.wait_ge(S["s_sS"], idx + 1)
                        if idx >= 2:
                            sc.wait_ge(S["s_av"], idx - 1)
                        sc.activation(
                            P_sb[:, par, :],
                            ps[:, par * 1024:(par + 1) * 1024],
                            AF.Exp, scale=ISCALE,
                        ).then_inc(S["s_exp"], 1)

            @blk.vector
            def _(v):
                v.memset(epsv[:, :], EPS)
                for s4 in range(4):
                    v.wait_ge(S["s_peq"], 16 * (s4 + 1))
                    for h in range(NHC):
                        v.tensor_copy(
                            qT[:, h, s4 * 1024:(s4 + 1) * 1024], bank(2 * h, 2)
                        ).then_inc(S["s_qev"], 1)
                for h in range(NHC):
                    for kt in range(NKT):
                        j = NKT * h + kt
                        v.wait_ge(S["s_vt"], j + 1)
                        v.tensor_copy(v_sb[:, h, kt, :], bank_bf(j % 2)).then_inc(S["s_vte"], 1)

                def emit_otn(hp):
                    v.wait_ge(S["s_zbc"], 32 * hp + 16)
                    if hp >= 1:
                        v.wait_ge(S["s_ag"], 32 * hp)
                    ins = None
                    for q2 in range(4):
                        if q2 == 2:
                            ins.then_inc(S["s_otA"], 1)
                            v.wait_ge(S["s_zbc"], 32 * hp + 32)
                            v.wait_ge(S["s_ag"], 32 * hp + 16)
                        elif q2 > 0:
                            v.drain()
                        ins = v.tensor_mul(
                            otn[:, (q2 % 2) * 1024:(q2 % 2 + 1) * 1024],
                            ot_raw[:, hp % 2, q2 * 1024:(q2 + 1) * 1024],
                            zinvb[:, (q2 % 2) * 1024:(q2 % 2 + 1) * 1024],
                        )
                    ins.then_inc(S["s_otn"], 1)

                for gidx in range(16):
                    h, qq = gidx // 4, gidx % 4
                    po = gidx % 2
                    for kt in range(NKT):
                        idx = 32 * gidx + kt
                        v.wait_ge(S["s_exp"], idx + 1)
                        v.tensor_scalar(
                            P_sb[:, kt % 2, :], P_sb[:, kt % 2, :], 1.0, None,
                            ALU.mult, ALU.add, accum_out=zp[:, h, qq, kt:kt + 1].opt(),
                        ).then_inc(S["s_zac"], 1)
                    v.wait_ge(S["s_av"], 32 * (gidx + 1))
                    v.tensor_copy(
                        ot_raw[:, h % 2, qq * 1024:(qq + 1) * 1024], bank(4 + 2 * po, 2)
                    ).then_inc(S["s_otev"], 1)
                    if qq == 3:
                        v.tensor_add(zs[:, :], zp[:, h, 0, :], zp[:, h, 1, :])
                        v.drain()
                        v.tensor_add(zs[:, :], zs[:, :], zp[:, h, 2, :])
                        v.drain()
                        v.tensor_add(zs[:, :], zs[:, :], zp[:, h, 3, :])
                        v.drain()
                        v.reciprocal(zinv_s[:, :], zs[:, :])
                        v.drain()
                        ins = None
                        for pb in range(4):
                            if pb > 0:
                                v.drain()
                            ins = v.transpose(
                                zrowt[0:32, 32 * pb:32 * (pb + 1)],
                                zinv_s[32 * pb:32 * (pb + 1), 0:32],
                            )
                        ins.then_inc(S["s_zrow"], 1)
                    if qq == 0 and gidx >= 4:
                        emit_otn(gidx // 4 - 1)
                emit_otn(3)

            @blk.sync
            def _(sy):
                pid = sy.partition_id()
                regs["r4"] = sy.compute_val(pid % 4)
                sy.dma_start(out=eye32[:, :], in_=eye).then_inc(S["s_bo"], 16)
                for h in range(NHC):
                    sy.wait_ge(S["s_zrow"], h + 1)
                    sy.dma_start(
                        out=zdr.ap()[h].rearrange("(kt p) -> kt p", p=128),
                        in_=zrowt[0:32, :],
                    ).then_inc(S["s_zw"], 16)
                    sy.wait_ge(S["s_zw"], 16 * (h + 1))
                    if h >= 1:
                        sy.wait_ge(S["s_otn"], h)
                    sy.dma_start(
                        out=zinvb[:, :],
                        in_=zdr.ap()[h][0:N // 2].partition_broadcast(128),
                    ).then_inc(S["s_zbc"], 16)
                    sy.wait_ge(S["s_otA"], h + 1)
                    sy.dma_start(
                        out=zinvb[:, :],
                        in_=zdr.ap()[h][N // 2:N].partition_broadcast(128),
                    ).then_inc(S["s_zbc"], 16)

        with (
            nc.sbuf_tensor("wg_sb", [128, 3, NDC, 128], BF16) as wg_sb,
            nc.sbuf_tensor("wu_sb", [128, 3, NDC, 128], BF16) as wu_sb,
            nc.sbuf_tensor("h1T", [128, NDC, MINE], BF16) as h1T,
        ):
            with (
                nc.sbuf_tensor("otg", [128, NDC, MINE], BF16) as otg,
                nc.sbuf_tensor("xres_sb", [128, 2, D], F32) as xres_sb,
                nc.sbuf_tensor("bo_bc", [128, D], F32) as bo_bc,
                nc.sbuf_tensor("z1", [128, D], F32) as z1,
                nc.sbuf_tensor("h1f", [128, 2, D], F32) as h1f,
                nc.Block() as b2,
            ):
                @b2.sync
                def _(sy):
                    def otg_load(hl):
                        for r in range(4):
                            sy.dma_start(
                                out=otg[:, 4 * r + hl, :],
                                in_=ag_recv[hl].ap()[r * 128:(r + 1) * 128, bass.ts(regs["r4"], MINE)],
                            ).then_inc(S[f"s_otg{hl}"], 16)
                    for hl in range(3):
                        sy.wait_ge(S["s_cc"], hl + 1)
                        otg_load(hl)
                    sy.dma_start(out=bo_bc[:, :], in_=bo.partition_broadcast(128)).then_inc(S["s_bo"], 16)
                    sy.wait_ge(S["s_cc"], NHC)
                    otg_load(3)
                    for tt in range(NTT):
                        if tt >= 2:
                            sy.wait_ge(S["s_h1f"], tt - 1)
                        sy.dma_start(
                            out=xres_sb[:, tt % 2, :],
                            in_=xres[tt * 128:(tt + 1) * 128, :],
                        ).then_inc(S[f"s_xr{tt % 2}"], 16)

                @b2.tensor
                def _(t):
                    t.wait_ge(S["s_exp"], 512)
                    t.wait_ge(S["s_otev"], 16)

                    def emit_tr(tt_src):
                        t.wait_ge(S["s_h1f"], tt_src + 1)
                        for j16 in range(16):
                            jj = 16 * tt_src + j16
                            if jj == 0:
                                pass
                            if jj >= 2:
                                t.wait_ge(S["s_tpe"], jj - 1)
                            t.transpose(
                                bank32(4 + jj % 2),
                                h1f[:, tt_src % 2, j16 * 128:(j16 + 1) * 128],
                                eye32[:, :],
                            ).then_inc(S["s_tp"], 1)

                    dc_order = [d for d in range(NDC) if d % 4 != 3] + [d for d in range(NDC) if d % 4 == 3]
                    for tt in range(NTT):
                        if tt == 0:
                            t.wait_ge(S["s_wo"], 16 * 4)
                        if tt >= 1:
                            t.wait_ge(S["s_ypsfree"], tt)
                        ins = None
                        for di, dc in enumerate(dc_order):
                            if tt == 0:
                                t.wait_ge(S[f"s_otg{dc % 4}"], 16 * 4)
                            for half in range(4):
                                ins = t.matmul(
                                    bank(half),
                                    otg[:, dc, tt * 128:(tt + 1) * 128],
                                    wo_view[:, dc, 512 * half:512 * (half + 1)],
                                    start=(di == 0), stop=(di == NDC - 1),
                                )
                        ins.then_inc(S["s_wope"], 1)
                        if tt >= 1:
                            emit_tr(tt - 1)
                    emit_tr(NTT - 1)

                @b2.vector
                def _(v):
                    def emit_trev(tt_src):
                        for j16 in range(16):
                            jj = 16 * tt_src + j16
                            v.wait_ge(S["s_tp"], jj + 1)
                            v.tensor_copy(
                                h1T[:, j16, tt_src * 128:(tt_src + 1) * 128],
                                bank32(4 + jj % 2),
                            ).then_inc(S["s_tpe"], 1)

                    for tt in range(NTT):
                        v.wait_ge(S["s_wope"], tt + 1)
                        if tt == 0:
                            v.wait_ge(S["s_bo"], 32)
                        v.scalar_tensor_tensor(
                            z1[:, :], ps[:, 0:2048], 1.0, bo_bc[:, :], ALU.mult, ALU.add
                        ).then_inc(S["s_ypsfree"], 1)
                        v.drain()
                        v.wait_ge(S[f"s_xr{tt % 2}"], 16 * (tt // 2 + 1))
                        v.tensor_add(z1[:, :], z1[:, :], xres_sb[:, tt % 2, :])
                        v.drain()
                        v.reduce_sum(rs[:, :], z1[:, :], axis=AX.X)
                        v.drain()
                        v.tensor_scalar_mul(negmean[:, :], rs[:, :], -1.0 / D)
                        v.drain()
                        v.tensor_scalar_add(z1[:, :], z1[:, :], negmean[:, :])
                        v.drain()
                        if tt >= 2:
                            v.wait_ge(S[f"s_h1d{tt % 2}"], 16 * (tt // 2))
                            v.wait_ge(S["s_tp"], 16 * (tt - 1))
                        v.scalar_tensor_tensor(
                            h1f[:, tt % 2, :], z1[:, :], 1.0, z1[:, :], ALU.mult, ALU.mult,
                            accum_out=ssq[:, :],
                        )
                        v.drain()
                        v.tensor_scalar_mul(var_[:, :], ssq[:, :], 1.0 / D).then_inc(S["s_var1"], 1)
                        v.wait_ge(S["s_actr"], tt + 1)
                        v.tensor_scalar_mul(h1f[:, tt % 2, :], z1[:, :], rstd[:, :]).then_inc(S["s_h1f"], 1)
                        v.drain()
                        if tt >= 1:
                            emit_trev(tt - 1)
                    emit_trev(NTT - 1)

                @b2.scalar
                def _(sc):
                    for tt in range(NTT):
                        sc.wait_ge(S["s_var1"], tt + 1)
                        sc.activation(lnv[:, :], var_[:, :], AF.Ln, bias=epsv[:, :])
                        sc.drain()
                        sc.activation(rstd[:, :], lnv[:, :], AF.Exp, scale=-0.5).then_inc(S["s_actr"], 1)
                        sc.drain()
                        sc.wait_ge(S["s_h1f"], tt + 1)
                        sc.dma_start(
                            out=h1d.ap()[tt * 128:(tt + 1) * 128, :],
                            in_=h1f[:, tt % 2, :],
                        ).then_inc(S[f"s_h1d{tt % 2}"], 16)

            with (
                nc.sbuf_tensor("sg", [128, 2, 1024], F32) as sg,
                nc.sbuf_tensor("gtw", [128, 3, 1024], BF16) as gtw,
                nc.sbuf_tensor("wdd", [128, 3, D], BF16) as wdd,
                nc.sbuf_tensor("h1t_sb", [128, 2, D], F32) as h1t_sb,
                nc.sbuf_tensor("outst", [128, 2, D], F32) as outst,
                nc.sbuf_tensor("z2", [128, 2, D], F32) as z2,
                nc.Block() as b3,
            ):
                @b3.gpsimd
                def _(g):
                    def load_wgu(ht):
                        for w_ap, w_sb in ((wg, wg_sb), (wu, wu_sb)):
                            g.dma_start(
                                out=w_sb[:, ht % 3, :, :].rearrange("p dc c -> p (dc c)"),
                                in_=w_ap[ht].rearrange("p dc c -> p (dc c)"),
                            ).then_inc(S[f"s_wg{ht % 3}"], 16)

                    load_wgu(0)
                    load_wgu(1)
                    load_wgu(2)
                    for ht in range(NHT):
                        if ht + 3 < NHT:
                            g.wait_ge(S["s_upe"], ht + 1)
                            load_wgu(ht + 3)

                @b3.tensor
                def _(t):
                    for ht in range(NHT):
                        t.wait_ge(S[f"s_wg{ht % 3}"], 32 * (ht // 3 + 1))
                        if ht == 0:
                            t.wait_ge(S["s_tpe"], 128)
                        if ht >= 2:
                            t.wait_ge(S["s_silu"], ht - 1)
                        ins = None
                        for dc in range(NDC):
                            for half in range(2):
                                ins = t.matmul(
                                    bank(2 * (ht % 2) + half),
                                    wg_sb[:, ht % 3, dc, :],
                                    h1T[:, dc, 512 * half:512 * (half + 1)],
                                    start=(dc == 0), stop=(dc == NDC - 1),
                                )
                        ins.then_inc(S["s_gpe"], 1)
                        if ht >= 2:
                            t.wait_ge(S["s_gt"], ht - 1)
                        ins = None
                        for dc in range(NDC):
                            for half in range(2):
                                ins = t.matmul(
                                    bank(4 + 2 * (ht % 2) + half),
                                    wu_sb[:, ht % 3, dc, :],
                                    h1T[:, dc, 512 * half:512 * (half + 1)],
                                    start=(dc == 0), stop=(dc == NDC - 1),
                                )
                        ins.then_inc(S["s_upe"], 1)
                    t.wait_ge(S["s_silu"], NHT)
                    t.wait_ge(S["s_gt"], NHT)
                    for p4 in range(4):
                        t.wait_ge(S[f"s_gtr{p4 % 2}"], 128 * (p4 // 2 + 1))
                        if p4 >= 1:
                            t.wait_ge(S["s_y2free"], 2 * p4)
                        for hc in range(NHT):
                            gidx2 = 64 * p4 + hc
                            t.wait_ge(S[f"s_wdd{gidx2 % 3}"], 16 * (gidx2 // 3 + 1))
                            ins = None
                            for side in range(2):
                                for half in range(4):
                                    ins = t.matmul(
                                        bank(4 * side + half),
                                        gtp_view[:, p4 % 2, hc, 128 * side:128 * (side + 1)],
                                        wdd[:, gidx2 % 3, 512 * half:512 * (half + 1)],
                                        start=(hc == 0), stop=(hc == NHT - 1),
                                    )
                            ins.then_inc(S["s_dhc"], 1)

                @b3.scalar
                def _(sc):
                    for ht in range(NHT):
                        sc.wait_ge(S["s_gpe"], ht + 1)
                        if ht >= 2:
                            sc.wait_ge(S["s_gt"], ht - 1)
                        sc.activation(
                            sg[:, ht % 2, :],
                            ps[:, (ht % 2) * 1024:(ht % 2 + 1) * 1024],
                            AF.Silu,
                        ).then_inc(S["s_silu"], 1)
                    sc.wait_ge(S["s_gtw0"], 16 * 22)
                    sc.wait_ge(S["s_gtw1"], 16 * 21)
                    sc.wait_ge(S["s_gtw2"], 16 * 21)
                    sc.wait_ge(S["s_wdw0"], 16 * NHT)
                    for p4 in range(4):
                        for hc in range(NHT):
                            gidx2 = 64 * p4 + hc
                            if gidx2 >= 3:
                                sc.wait_ge(S["s_dhc"], gidx2 - 2)
                            sc.dma_start(
                                out=wdd[:, gidx2 % 3, :],
                                in_=wdb.ap()[hc * 128:(hc + 1) * 128, :],
                            ).then_inc(S[f"s_wdd{gidx2 % 3}"], 16)
                        for tt in (2 * p4, 2 * p4 + 1):
                            sc.wait_ge(S["s_var2"], tt + 1)
                            sc.activation(lnv[:, :], var_[:, :], AF.Ln, bias=epsv[:, :])
                            sc.drain()
                            sc.activation(rstd[:, :], lnv[:, :], AF.Exp, scale=-0.5).then_inc(S["s_actr2"], 1)
                            sc.drain()
                            sc.wait_ge(S["s_ovr"], tt + 1)
                            sc.dma_start(
                                out=out[tt * 128:(tt + 1) * 128, :],
                                in_=outst[:, tt % 2, :],
                            ).then_inc(S[f"s_out{tt % 2}"], 16)

                @b3.vector
                def _(v):
                    for ht in range(NHT):
                        v.wait_ge(S["s_silu"], ht + 1)
                        v.wait_ge(S["s_upe"], ht + 1)
                        if ht >= 3:
                            v.wait_ge(S[f"s_gtw{ht % 3}"], 16 * (ht // 3))
                        v.tensor_mul(
                            gtw[:, ht % 3, :],
                            sg[:, ht % 2, :],
                            ps[:, (4 + 2 * (ht % 2)) * 512:(6 + 2 * (ht % 2)) * 512],
                        ).then_inc(S["s_gt"], 1)
                    for p4 in range(4):
                        v.wait_ge(S["s_dhc"], 64 * (p4 + 1))
                        for tt in (2 * p4, 2 * p4 + 1):
                            v.wait_ge(S[f"s_h1ld{tt % 2}"], 16 * (tt // 2 + 1))
                            v.scalar_tensor_tensor(
                                z2[:, tt % 2, :], ps[:, (tt % 2) * 2048:(tt % 2 + 1) * 2048], 1.0,
                                h1t_sb[:, tt % 2, :], ALU.mult, ALU.add,
                            ).then_inc(S["s_y2free"], 1)
                            v.drain()
                        for tt in (2 * p4, 2 * p4 + 1):
                            v.reduce_sum(rs[:, :], z2[:, tt % 2, :], axis=AX.X)
                            v.drain()
                            v.tensor_scalar_mul(negmean[:, :], rs[:, :], -1.0 / D)
                            v.drain()
                            v.tensor_scalar_add(z2[:, tt % 2, :], z2[:, tt % 2, :], negmean[:, :])
                            v.drain()
                            if tt >= 2:
                                v.wait_ge(S[f"s_out{tt % 2}"], 16 * (tt // 2))
                            v.scalar_tensor_tensor(
                                outst[:, tt % 2, :], z2[:, tt % 2, :], 1.0, z2[:, tt % 2, :],
                                ALU.mult, ALU.mult, accum_out=ssq[:, :],
                            )
                            v.drain()
                            v.tensor_scalar_mul(var_[:, :], ssq[:, :], 1.0 / D).then_inc(S["s_var2"], 1)
                            v.wait_ge(S["s_actr2"], tt + 1)
                            v.tensor_scalar_mul(outst[:, tt % 2, :], z2[:, tt % 2, :], rstd[:, :]).then_inc(S["s_ovr"], 1)
                            v.drain()

                @b3.sync
                def _(sy):
                    for ht in range(NHT):
                        sy.wait_ge(S["s_gt"], ht + 1)
                        sy.dma_start(
                            out=gtd.ap()[ht * 128:(ht + 1) * 128, :],
                            in_=gtw[:, ht % 3, :],
                        ).then_inc(S[f"s_gtw{ht % 3}"], 16)
                    sy.wait_ge(S["s_wope"], NTT)
                    sy.wait_ge(S["s_gtw0"], 16 * 22)
                    sy.wait_ge(S["s_gtw1"], 16 * 21)
                    sy.wait_ge(S["s_gtw2"], 16 * 21)
                    sy.wait_ge(S["s_wdw0"], 16 * NHT)
                    sy.wait_ge(S["s_h1d0"], 16 * (NTT // 2))
                    sy.wait_ge(S["s_h1d1"], 16 * (NTT // 2))
                    gtd_r = gtd.ap().rearrange("(hc k) t -> k hc t", k=128)
                    for p4 in range(4):
                        if p4 >= 2:
                            sy.wait_ge(S["s_dhc"], 64 * (p4 - 1))
                        for oc in range(8):
                            sy.dma_start(
                                out=gtp_view[:, p4 % 2, 8 * oc:8 * (oc + 1), :],
                                in_=gtd_r[:, 8 * oc:8 * (oc + 1), 256 * p4:256 * (p4 + 1)],
                            ).then_inc(S[f"s_gtr{p4 % 2}"], 16)
                        for tt in (2 * p4, 2 * p4 + 1):
                            if tt >= 2:
                                sy.wait_ge(S["s_y2free"], tt - 1)
                            sy.dma_start(
                                out=h1t_sb[:, tt % 2, :],
                                in_=h1d.ap()[tt * 128:(tt + 1) * 128, :],
                            ).then_inc(S[f"s_h1ld{tt % 2}"], 16)
                    sy.wait_ge(S["s_out0"], 16 * (NTT // 2))
                    sy.wait_ge(S["s_out1"], 16 * (NTT // 2))

    return nc


_NC_CACHE = []


def kernel(L0_attn_x, L0_attn_Wq, L0_attn_Wo, L0_attn_bo,
           L0_ffn_w_gate, L0_ffn_w_up, L0_ffn_w_down):
    x = np.asarray(L0_attn_x, dtype=np.float32)
    Wq = np.asarray(L0_attn_Wq, dtype=np.float32)
    Wo = np.ascontiguousarray(np.asarray(L0_attn_Wo, dtype=np.float32))
    bov = np.ascontiguousarray(np.asarray(L0_attn_bo, dtype=np.float32))
    wgt = np.ascontiguousarray(
        np.asarray(L0_ffn_w_gate, dtype=np.float32).reshape(NHT, 128, NDC, 128).transpose(0, 3, 2, 1))
    wut = np.ascontiguousarray(
        np.asarray(L0_ffn_w_up, dtype=np.float32).reshape(NHT, 128, NDC, 128).transpose(0, 3, 2, 1))
    wdT = np.ascontiguousarray(np.asarray(L0_ffn_w_down, dtype=np.float32).T)
    eye = np.eye(128, dtype=np.float32)
    xT = [np.ascontiguousarray(x[b].T) for b in range(2)]

    in_maps = []
    for c in range(8):
        b, r4 = c // 4, c % 4
        wqt = np.ascontiguousarray(
            Wq[:, r4 * 512:(r4 + 1) * 512].reshape(NDC, 128, NHC, 128).transpose(1, 0, 2, 3))
        in_maps.append(dict(
            xt=xT[b],
            xres=np.ascontiguousarray(x[b, r4 * MINE:(r4 + 1) * MINE]),
            wq=wqt,
            wo=Wo, bo=bov, wg=wgt, wu=wut, wd=wdT, eye=eye,
        ))

    if not _NC_CACHE:
        _NC_CACHE.append(build())
    nc = _NC_CACHE[0]
    res = run_bass_kernel_spmd(nc, in_maps, core_ids=list(range(8)))
    pieces = [res.results[c]["out"] for c in range(8)]
    return np.concatenate(pieces, axis=0).reshape(2, N, D)


if __name__ == "__main__":
    rng = np.random.default_rng(1)
    x = rng.standard_normal((2, N, D), dtype=np.float32)
    Wq = (rng.standard_normal((D, D), dtype=np.float32) * D ** -0.5).astype(np.float32)
    Wo = (rng.standard_normal((D, D), dtype=np.float32) * D ** -0.5).astype(np.float32)
    bov = rng.standard_normal(D).astype(np.float32) * 0.1
    wgm = (rng.uniform(-1, 1, (HID, D)) / np.sqrt(D)).astype(np.float32)
    wum = (rng.uniform(-1, 1, (HID, D)) / np.sqrt(D)).astype(np.float32)
    wdm = (rng.uniform(-1, 1, (D, HID)) / np.sqrt(HID)).astype(np.float32)

    def ref():
        def ln(a):
            m = a.mean(-1, keepdims=True)
            v = a.var(-1, keepdims=True)
            return (a - m) / np.sqrt(v + EPS)
        q = x @ Wq
        qh = q.reshape(2, N, 16, HD).transpose(0, 2, 1, 3)
        s = np.einsum("bhnd,bhmd->bhnm", qh, qh) * HD ** -0.5
        s = s - s.max(-1, keepdims=True)
        a = np.exp(s)
        a /= a.sum(-1, keepdims=True)
        y = np.einsum("bhnm,bhmd->bhnd", a, qh).transpose(0, 2, 1, 3).reshape(2, N, D)
        h = ln(x + y @ Wo + bov)
        gate = h @ wgm.T
        up = h @ wum.T
        gated = gate / (1 + np.exp(-gate)) * up
        return ln(h + gated @ wdm.T)

    got = kernel(L0_attn_x=x, L0_attn_Wq=Wq, L0_attn_Wo=Wo, L0_attn_bo=bov,
                 L0_ffn_w_gate=wgm, L0_ffn_w_up=wum, L0_ffn_w_down=wdm)
    want = ref()
    err = np.sqrt(np.mean((got - want) ** 2) / np.mean(want ** 2))
    print("rel l2:", err)
    print("max abs:", np.abs(got - want).max())


# version 05
# speedup vs baseline: 1.0669x; 1.0669x over previous
import numpy as np
import concourse.bass as bass
import concourse.mybir as mybir
from concourse.bass_utils import run_bass_kernel_spmd

F32 = mybir.dt.float32
BF16 = mybir.dt.bfloat16
AF = mybir.ActivationFunctionType
ALU = mybir.AluOpType
AX = mybir.AxisListType

N = 4096
D = 2048
HD = 128
NHC = 4
MINE = 1024
HID = 8192
NDC = D // 128
NHT = HID // 128
NKT = N // 128
NTT = MINE // 128
ISCALE = float(1.0 / np.sqrt(HD))
EPS = 1e-5
GROUPS = [[0, 1, 2, 3], [4, 5, 6, 7]]


def build():
    nc = bass.Bass("TRN2", target_bir_lowering=False)

    xt = nc.dram_tensor("xt", [D, N], F32, kind="ExternalInput").ap()
    xres = nc.dram_tensor("xres", [MINE, D], F32, kind="ExternalInput").ap()
    wq = nc.dram_tensor("wq", [128, NDC, NHC, HD], F32, kind="ExternalInput").ap()
    wo = nc.dram_tensor("wo", [D, D], F32, kind="ExternalInput").ap()
    bo = nc.dram_tensor("bo", [D], F32, kind="ExternalInput").ap()
    wg = nc.dram_tensor("wg", [NHT, 128, NDC, 128], F32, kind="ExternalInput").ap()
    wu = nc.dram_tensor("wu", [NHT, 128, NDC, 128], F32, kind="ExternalInput").ap()
    wd = nc.dram_tensor("wd", [HID, D], F32, kind="ExternalInput").ap()
    eye = nc.dram_tensor("eye", [128, 128], F32, kind="ExternalInput").ap()
    out = nc.dram_tensor("out", [MINE, D], F32, kind="ExternalOutput").ap()

    ag_send = [nc.dram_tensor(f"ag_send{h}", [HD, N], BF16) for h in range(NHC)]
    ag_recv = [nc.dram_tensor(f"ag_recv{h}", [4 * HD, N], BF16) for h in range(NHC)]
    zdr = nc.dram_tensor("zdr", [NHC, N], F32)
    h1d = nc.dram_tensor("h1d", [MINE, D], F32)
    gtd = nc.dram_tensor("gtd", [HID, MINE], BF16)
    wdb = nc.dram_tensor("wdb", [HID, D], BF16)

    sem_names = [
        "s_pre", "s_xt0", "s_xt1", "s_xt2", "s_xt3",
        "s_qev", "s_peq", "s_vt", "s_vte",
        "s_sS", "s_exp", "s_zac", "s_av", "s_otev", "s_zrow", "s_zw", "s_zbc", "s_otA", "s_otn",
        "s_ag", "s_cc",
        "s_wo", "s_otg0", "s_otg1", "s_otg2", "s_otg3", "s_xr0", "s_xr1", "s_bo", "s_wope", "s_ypsfree",
        "s_var1", "s_actr", "s_h1f", "s_h1d0", "s_h1d1", "s_tp", "s_tpe",
        "s_wg0", "s_wg1", "s_wg2", "s_gpe", "s_upe", "s_silu", "s_gt",
        "s_gtw0", "s_gtw1", "s_gtw2",
        "s_wdr0", "s_wdr1", "s_wdr2", "s_wdr3",
        "s_wdw0", "s_wdw1", "s_wdw2", "s_wdw3",
        "s_gtr0", "s_gtr1", "s_wdd0", "s_wdd1", "s_wdd2", "s_dhc", "s_y2free",
        "s_var2", "s_actr2", "s_h1ld0", "s_h1ld1", "s_ovr", "s_out0", "s_out1",
    ]

    import contextlib
    with contextlib.ExitStack() as est:
        S = {n: est.enter_context(nc.semaphore(n)) for n in sem_names}
        ps = est.enter_context(nc.psum_tensor("ps", [128, 4096], F32))

        def bank(i, n=1):
            return ps[:, 512 * i: 512 * (i + n)]

        def bank_bf(i):
            return ps[:, 512 * i: 512 * i + 64].bitcast(BF16)

        def bank32(i):
            return ps[:, 512 * i: 512 * i + 128]

        eye_bf = est.enter_context(nc.sbuf_tensor("eye_bf", [128, 128], BF16))
        eye32 = est.enter_context(nc.sbuf_tensor("eye32", [128, 128], F32))
        big64 = est.enter_context(nc.sbuf_tensor("big64", [128, 2 * NHT * 256], BF16))
        wo_view = big64[:, :].rearrange("p (dc n) -> p dc n", dc=NDC)
        gtp_view = big64[:, :].rearrange("p (b hc m) -> p b hc m", b=2, hc=NHT)
        rs = est.enter_context(nc.sbuf_tensor("rs", [128, 1], F32))
        negmean = est.enter_context(nc.sbuf_tensor("negmean", [128, 1], F32))
        ssq = est.enter_context(nc.sbuf_tensor("ssq", [128, 1], F32))
        var_ = est.enter_context(nc.sbuf_tensor("var_", [128, 1], F32))
        lnv = est.enter_context(nc.sbuf_tensor("lnv", [128, 1], F32))
        rstd = est.enter_context(nc.sbuf_tensor("rstd", [128, 1], F32))
        epsv = est.enter_context(nc.sbuf_tensor("epsv", [128, 1], F32))
        wdstage = est.enter_context(nc.sbuf_tensor("wdstage", [128, 1, D], BF16))

        regs = {}

        with (
            nc.sbuf_tensor("qT", [128, NHC, N], BF16) as qT,
            nc.sbuf_tensor("v_sb", [128, NHC, NKT, HD], BF16) as v_sb,
            nc.sbuf_tensor("xt_sb", [128, 3, 1024], BF16) as xt_sb,
            nc.sbuf_tensor("wq_sb", [128, NDC, NHC, HD], BF16) as wq_sb,
            nc.sbuf_tensor("P_sb", [128, 2, 1024], BF16) as P_sb,
            nc.sbuf_tensor("ot_raw", [128, 2, N], F32) as ot_raw,
            nc.sbuf_tensor("zp", [128, NHC, 4, NKT], F32) as zp,
            nc.sbuf_tensor("zs", [128, NKT], F32) as zs,
            nc.sbuf_tensor("zinv_s", [128, NKT], F32) as zinv_s,
            nc.sbuf_tensor("zrowt", [32, 128], F32) as zrowt,
            nc.sbuf_tensor("zinvb", [128, N // 2], F32) as zinvb,
            nc.sbuf_tensor("otn", [128, N // 2], BF16) as otn,
            nc.Block() as blk,
        ):
            @blk.gpsimd
            def _(g):
                g.dma_start(out=wq_sb[:, :, :, :].rearrange("p dc h c -> p (dc h c)"),
                            in_=wq.rearrange("p dc h c -> p (dc h c)")).then_inc(S["s_pre"], 16)
                g.dma_start(out=eye_bf[:, :], in_=eye).then_inc(S["s_pre"], 16)
                for s4 in range(4):
                    for dc in range(NDC):
                        j = 16 * s4 + dc
                        if j >= 3:
                            g.wait_ge(S["s_peq"], j - 2)
                        g.dma_start(
                            out=xt_sb[:, j % 3, :],
                            in_=xt[dc * 128:(dc + 1) * 128, s4 * 1024:(s4 + 1) * 1024],
                        ).then_inc(S[f"s_xt{j % 3}"], 16)
                for dq in range(4):
                    g.dma_start(
                        out=wo_view[:, dq * 4:(dq + 1) * 4, :],
                        in_=wo[dq * 512:(dq + 1) * 512, :].rearrange("(dc p) n -> p dc n", p=128),
                    ).then_inc(S["s_wo"], 16)
                def wd_job(lo, hi):
                    for ht2 in range(lo, hi):
                        if ht2 >= 1:
                            g.wait_ge(S["s_wdw0"], 16 * ht2)
                        g.dma_start(
                            out=wdstage[:, 0, :],
                            in_=wd[ht2 * 128:(ht2 + 1) * 128, :],
                        ).then_inc(S["s_wdr0"], 16)
                        g.wait_ge(S["s_wdr0"], 16 * (ht2 + 1))
                        g.dma_start(
                            out=wdb.ap()[ht2 * 128:(ht2 + 1) * 128, :],
                            in_=wdstage[:, 0, :],
                        ).then_inc(S["s_wdw0"], 16)

                for h in range(NHC):
                    wd_job(16 * h, 16 * (h + 1))
                    g.wait_ge(S["s_otA"], h + 1)
                    g.dma_start(out=ag_send[h].ap()[:, 0:N // 2], in_=otn[:, :]).then_inc(S["s_ag"], 16)
                    g.wait_ge(S["s_otn"], h + 1)
                    g.dma_start(out=ag_send[h].ap()[:, N // 2:N], in_=otn[:, :]).then_inc(S["s_ag"], 16)
                    g.wait_ge(S["s_ag"], 32 * (h + 1))
                    g.collective_compute(
                        "AllGather", ALU.bypass, replica_groups=GROUPS,
                        ins=[ag_send[h].ap()], outs=[ag_recv[h].ap()],
                    ).then_inc(S["s_cc"], 1)


            @blk.tensor
            def _(t):
                t.wait_ge(S["s_pre"], 32)
                for s4 in range(4):
                    for dc in range(NDC):
                        j = 16 * s4 + dc
                        t.wait_ge(S[f"s_xt{j % 3}"], 16 * (j // 3 + 1))
                        if dc == 0 and s4 > 0:
                            t.wait_ge(S["s_qev"], 4 * s4)
                        ins = None
                        for h in range(NHC):
                            for half in range(2):
                                ins = t.matmul(
                                    bank(2 * h + half),
                                    wq_sb[:, dc, h, :],
                                    xt_sb[:, j % 3, 512 * half:512 * (half + 1)],
                                    start=(dc == 0), stop=(dc == NDC - 1),
                                )
                        ins.then_inc(S["s_peq"], 1)
                for h in range(NHC):
                    for kt in range(NKT):
                        j = NKT * h + kt
                        if j == 0:
                            t.wait_ge(S["s_qev"], 16)
                        if j >= 2:
                            t.wait_ge(S["s_vte"], j - 1)
                        t.transpose(
                            bank_bf(j % 2),
                            qT[:, h, kt * 128:(kt + 1) * 128],
                            eye_bf[:, :],
                        ).then_inc(S["s_vt"], 1)
                for gidx in range(16):
                    h, qq = gidx // 4, gidx % 4
                    po = gidx % 2

                    def emit_S(kt, gidx=gidx, h=h, qq=qq):
                        idx = 32 * gidx + kt
                        if idx == 0:
                            t.wait_ge(S["s_vte"], 128)
                        par = kt % 2
                        ins = None
                        for half in range(2):
                            ins = t.matmul(
                                bank(2 * par + half),
                                qT[:, h, kt * 128:(kt + 1) * 128],
                                qT[:, h, qq * 1024 + 512 * half: qq * 1024 + 512 * (half + 1)],
                                start=True, stop=True,
                            )
                        ins.then_inc(S["s_sS"], 1)

                    def emit_AV(kt, gidx=gidx, h=h, po=po):
                        idx = 32 * gidx + kt
                        t.wait_ge(S["s_exp"], idx + 1)
                        if kt == 0 and gidx >= 2:
                            t.wait_ge(S["s_otev"], gidx - 1)
                        ins = None
                        for half in range(2):
                            ins = t.matmul(
                                bank(4 + 2 * po + half),
                                v_sb[:, h, kt, :],
                                P_sb[:, kt % 2, 512 * half:512 * (half + 1)],
                                start=(kt == 0), stop=(kt == NKT - 1),
                            )
                        ins.then_inc(S["s_av"], 1)

                    emit_S(0)
                    emit_S(1)
                    for kt in range(NKT):
                        emit_AV(kt)
                        if kt + 2 < NKT:
                            emit_S(kt + 2)

            @blk.scalar
            def _(sc):
                for gidx in range(16):
                    h, qq = gidx // 4, gidx % 4
                    for kt in range(NKT):
                        idx = 32 * gidx + kt
                        par = kt % 2
                        sc.wait_ge(S["s_sS"], idx + 1)
                        if idx >= 2:
                            sc.wait_ge(S["s_av"], idx - 1)
                        sc.activation(
                            P_sb[:, par, :],
                            ps[:, par * 1024:(par + 1) * 1024],
                            AF.Exp, scale=ISCALE,
                            accum_out=zp[:, h, qq, kt:kt + 1].opt(),
                        ).then_inc(S["s_exp"], 1)

            @blk.vector
            def _(v):
                v.memset(epsv[:, :], EPS)
                for s4 in range(4):
                    v.wait_ge(S["s_peq"], 16 * (s4 + 1))
                    for h in range(NHC):
                        v.tensor_copy(
                            qT[:, h, s4 * 1024:(s4 + 1) * 1024], bank(2 * h, 2)
                        ).then_inc(S["s_qev"], 1)
                for h in range(NHC):
                    for kt in range(NKT):
                        j = NKT * h + kt
                        v.wait_ge(S["s_vt"], j + 1)
                        v.tensor_copy(v_sb[:, h, kt, :], bank_bf(j % 2)).then_inc(S["s_vte"], 1)

                def emit_otn(hp):
                    v.wait_ge(S["s_zbc"], 32 * hp + 16)
                    if hp >= 1:
                        v.wait_ge(S["s_ag"], 32 * hp)
                    ins = None
                    for q2 in range(4):
                        if q2 == 2:
                            ins.then_inc(S["s_otA"], 1)
                            v.wait_ge(S["s_zbc"], 32 * hp + 32)
                            v.wait_ge(S["s_ag"], 32 * hp + 16)
                        elif q2 > 0:
                            v.drain()
                        ins = v.tensor_mul(
                            otn[:, (q2 % 2) * 1024:(q2 % 2 + 1) * 1024],
                            ot_raw[:, hp % 2, q2 * 1024:(q2 + 1) * 1024],
                            zinvb[:, (q2 % 2) * 1024:(q2 % 2 + 1) * 1024],
                        )
                    ins.then_inc(S["s_otn"], 1)

                for gidx in range(16):
                    h, qq = gidx // 4, gidx % 4
                    po = gidx % 2
                    v.wait_ge(S["s_av"], 32 * (gidx + 1))
                    v.tensor_copy(
                        ot_raw[:, h % 2, qq * 1024:(qq + 1) * 1024], bank(4 + 2 * po, 2)
                    ).then_inc(S["s_otev"], 1)
                    if qq == 3:
                        v.tensor_add(zs[:, :], zp[:, h, 0, :], zp[:, h, 1, :])
                        v.drain()
                        v.tensor_add(zs[:, :], zs[:, :], zp[:, h, 2, :])
                        v.drain()
                        v.tensor_add(zs[:, :], zs[:, :], zp[:, h, 3, :])
                        v.drain()
                        v.reciprocal(zinv_s[:, :], zs[:, :])
                        v.drain()
                        ins = None
                        for pb in range(4):
                            if pb > 0:
                                v.drain()
                            ins = v.transpose(
                                zrowt[0:32, 32 * pb:32 * (pb + 1)],
                                zinv_s[32 * pb:32 * (pb + 1), 0:32],
                            )
                        ins.then_inc(S["s_zrow"], 1)
                    if qq == 0 and gidx >= 4:
                        emit_otn(gidx // 4 - 1)
                emit_otn(3)

            @blk.sync
            def _(sy):
                pid = sy.partition_id()
                regs["r4"] = sy.compute_val(pid % 4)
                sy.dma_start(out=eye32[:, :], in_=eye).then_inc(S["s_bo"], 16)
                for h in range(NHC):
                    sy.wait_ge(S["s_zrow"], h + 1)
                    sy.dma_start(
                        out=zdr.ap()[h].rearrange("(kt p) -> kt p", p=128),
                        in_=zrowt[0:32, :],
                    ).then_inc(S["s_zw"], 16)
                    sy.wait_ge(S["s_zw"], 16 * (h + 1))
                    if h >= 1:
                        sy.wait_ge(S["s_otn"], h)
                    sy.dma_start(
                        out=zinvb[:, :],
                        in_=zdr.ap()[h][0:N // 2].partition_broadcast(128),
                    ).then_inc(S["s_zbc"], 16)
                    sy.wait_ge(S["s_otA"], h + 1)
                    sy.dma_start(
                        out=zinvb[:, :],
                        in_=zdr.ap()[h][N // 2:N].partition_broadcast(128),
                    ).then_inc(S["s_zbc"], 16)

        with (
            nc.sbuf_tensor("wg_sb", [128, 3, NDC, 128], BF16) as wg_sb,
            nc.sbuf_tensor("wu_sb", [128, 3, NDC, 128], BF16) as wu_sb,
            nc.sbuf_tensor("h1T", [128, NDC, MINE], BF16) as h1T,
        ):
            with (
                nc.sbuf_tensor("otg", [128, NDC, MINE], BF16) as otg,
                nc.sbuf_tensor("xres_sb", [128, 2, D], F32) as xres_sb,
                nc.sbuf_tensor("bo_bc", [128, D], F32) as bo_bc,
                nc.sbuf_tensor("z1", [128, D], F32) as z1,
                nc.sbuf_tensor("h1f", [128, 2, D], F32) as h1f,
                nc.Block() as b2,
            ):
                @b2.sync
                def _(sy):
                    def otg_load(hl):
                        for r in range(4):
                            sy.dma_start(
                                out=otg[:, 4 * r + hl, :],
                                in_=ag_recv[hl].ap()[r * 128:(r + 1) * 128, bass.ts(regs["r4"], MINE)],
                            ).then_inc(S[f"s_otg{hl}"], 16)
                    for hl in range(3):
                        sy.wait_ge(S["s_cc"], hl + 1)
                        otg_load(hl)
                    sy.dma_start(out=bo_bc[:, :], in_=bo.partition_broadcast(128)).then_inc(S["s_bo"], 16)
                    sy.wait_ge(S["s_cc"], NHC)
                    otg_load(3)
                    for tt in range(NTT):
                        if tt >= 2:
                            sy.wait_ge(S["s_h1f"], tt - 1)
                        sy.dma_start(
                            out=xres_sb[:, tt % 2, :],
                            in_=xres[tt * 128:(tt + 1) * 128, :],
                        ).then_inc(S[f"s_xr{tt % 2}"], 16)

                @b2.tensor
                def _(t):
                    t.wait_ge(S["s_exp"], 512)
                    t.wait_ge(S["s_otev"], 16)

                    def emit_tr(tt_src):
                        t.wait_ge(S["s_h1f"], tt_src + 1)
                        for j16 in range(16):
                            jj = 16 * tt_src + j16
                            if jj == 0:
                                pass
                            if jj >= 2:
                                t.wait_ge(S["s_tpe"], jj - 1)
                            t.transpose(
                                bank32(4 + jj % 2),
                                h1f[:, tt_src % 2, j16 * 128:(j16 + 1) * 128],
                                eye32[:, :],
                            ).then_inc(S["s_tp"], 1)

                    dc_order = [d for d in range(NDC) if d % 4 != 3] + [d for d in range(NDC) if d % 4 == 3]
                    for tt in range(NTT):
                        if tt == 0:
                            t.wait_ge(S["s_wo"], 16 * 4)
                        if tt >= 1:
                            t.wait_ge(S["s_ypsfree"], tt)
                        ins = None
                        for di, dc in enumerate(dc_order):
                            if tt == 0:
                                t.wait_ge(S[f"s_otg{dc % 4}"], 16 * 4)
                            for half in range(4):
                                ins = t.matmul(
                                    bank(half),
                                    otg[:, dc, tt * 128:(tt + 1) * 128],
                                    wo_view[:, dc, 512 * half:512 * (half + 1)],
                                    start=(di == 0), stop=(di == NDC - 1),
                                )
                        ins.then_inc(S["s_wope"], 1)
                        if tt >= 1:
                            emit_tr(tt - 1)
                    emit_tr(NTT - 1)

                @b2.vector
                def _(v):
                    def emit_trev(tt_src):
                        for j16 in range(16):
                            jj = 16 * tt_src + j16
                            v.wait_ge(S["s_tp"], jj + 1)
                            v.tensor_copy(
                                h1T[:, j16, tt_src * 128:(tt_src + 1) * 128],
                                bank32(4 + jj % 2),
                            ).then_inc(S["s_tpe"], 1)

                    for tt in range(NTT):
                        v.wait_ge(S["s_wope"], tt + 1)
                        if tt == 0:
                            v.wait_ge(S["s_bo"], 32)
                        v.scalar_tensor_tensor(
                            z1[:, :], ps[:, 0:2048], 1.0, bo_bc[:, :], ALU.mult, ALU.add
                        ).then_inc(S["s_ypsfree"], 1)
                        v.drain()
                        v.wait_ge(S[f"s_xr{tt % 2}"], 16 * (tt // 2 + 1))
                        v.tensor_add(z1[:, :], z1[:, :], xres_sb[:, tt % 2, :])
                        v.drain()
                        v.reduce_sum(rs[:, :], z1[:, :], axis=AX.X)
                        v.drain()
                        v.tensor_scalar_mul(negmean[:, :], rs[:, :], -1.0 / D)
                        v.drain()
                        v.tensor_scalar_add(z1[:, :], z1[:, :], negmean[:, :])
                        v.drain()
                        if tt >= 2:
                            v.wait_ge(S[f"s_h1d{tt % 2}"], 16 * (tt // 2))
                            v.wait_ge(S["s_tp"], 16 * (tt - 1))
                        v.scalar_tensor_tensor(
                            h1f[:, tt % 2, :], z1[:, :], 1.0, z1[:, :], ALU.mult, ALU.mult,
                            accum_out=ssq[:, :],
                        )
                        v.drain()
                        v.tensor_scalar_mul(var_[:, :], ssq[:, :], 1.0 / D).then_inc(S["s_var1"], 1)
                        v.wait_ge(S["s_actr"], tt + 1)
                        v.tensor_scalar_mul(h1f[:, tt % 2, :], z1[:, :], rstd[:, :]).then_inc(S["s_h1f"], 1)
                        v.drain()
                        if tt >= 1:
                            emit_trev(tt - 1)
                    emit_trev(NTT - 1)

                @b2.scalar
                def _(sc):
                    for tt in range(NTT):
                        sc.wait_ge(S["s_var1"], tt + 1)
                        sc.activation(lnv[:, :], var_[:, :], AF.Ln, bias=epsv[:, :])
                        sc.drain()
                        sc.activation(rstd[:, :], lnv[:, :], AF.Exp, scale=-0.5).then_inc(S["s_actr"], 1)
                        sc.drain()
                        sc.wait_ge(S["s_h1f"], tt + 1)
                        sc.dma_start(
                            out=h1d.ap()[tt * 128:(tt + 1) * 128, :],
                            in_=h1f[:, tt % 2, :],
                        ).then_inc(S[f"s_h1d{tt % 2}"], 16)

            with (
                nc.sbuf_tensor("sg", [128, 2, 1024], F32) as sg,
                nc.sbuf_tensor("gtw", [128, 3, 1024], BF16) as gtw,
                nc.sbuf_tensor("wdd", [128, 3, D], BF16) as wdd,
                nc.sbuf_tensor("h1t_sb", [128, 2, D], F32) as h1t_sb,
                nc.sbuf_tensor("outst", [128, 2, D], F32) as outst,
                nc.sbuf_tensor("z2", [128, 2, D], F32) as z2,
                nc.Block() as b3,
            ):
                @b3.gpsimd
                def _(g):
                    def load_wgu(ht):
                        for w_ap, w_sb in ((wg, wg_sb), (wu, wu_sb)):
                            g.dma_start(
                                out=w_sb[:, ht % 3, :, :].rearrange("p dc c -> p (dc c)"),
                                in_=w_ap[ht].rearrange("p dc c -> p (dc c)"),
                            ).then_inc(S[f"s_wg{ht % 3}"], 16)

                    load_wgu(0)
                    load_wgu(1)
                    load_wgu(2)
                    for ht in range(NHT):
                        if ht + 3 < NHT:
                            g.wait_ge(S["s_upe"], ht + 1)
                            load_wgu(ht + 3)

                @b3.tensor
                def _(t):
                    for ht in range(NHT):
                        t.wait_ge(S[f"s_wg{ht % 3}"], 32 * (ht // 3 + 1))
                        if ht == 0:
                            t.wait_ge(S["s_tpe"], 128)
                        if ht >= 2:
                            t.wait_ge(S["s_silu"], ht - 1)
                        ins = None
                        for dc in range(NDC):
                            for half in range(2):
                                ins = t.matmul(
                                    bank(2 * (ht % 2) + half),
                                    wg_sb[:, ht % 3, dc, :],
                                    h1T[:, dc, 512 * half:512 * (half + 1)],
                                    start=(dc == 0), stop=(dc == NDC - 1),
                                )
                        ins.then_inc(S["s_gpe"], 1)
                        if ht >= 2:
                            t.wait_ge(S["s_gt"], ht - 1)
                        ins = None
                        for dc in range(NDC):
                            for half in range(2):
                                ins = t.matmul(
                                    bank(4 + 2 * (ht % 2) + half),
                                    wu_sb[:, ht % 3, dc, :],
                                    h1T[:, dc, 512 * half:512 * (half + 1)],
                                    start=(dc == 0), stop=(dc == NDC - 1),
                                )
                        ins.then_inc(S["s_upe"], 1)
                    t.wait_ge(S["s_silu"], NHT)
                    t.wait_ge(S["s_gt"], NHT)
                    for p4 in range(4):
                        t.wait_ge(S[f"s_gtr{p4 % 2}"], 128 * (p4 // 2 + 1))
                        if p4 >= 1:
                            t.wait_ge(S["s_y2free"], 2 * p4)
                        for hc in range(NHT):
                            gidx2 = 64 * p4 + hc
                            t.wait_ge(S[f"s_wdd{gidx2 % 3}"], 16 * (gidx2 // 3 + 1))
                            ins = None
                            for side in range(2):
                                for half in range(4):
                                    ins = t.matmul(
                                        bank(4 * side + half),
                                        gtp_view[:, p4 % 2, hc, 128 * side:128 * (side + 1)],
                                        wdd[:, gidx2 % 3, 512 * half:512 * (half + 1)],
                                        start=(hc == 0), stop=(hc == NHT - 1),
                                    )
                            ins.then_inc(S["s_dhc"], 1)

                @b3.scalar
                def _(sc):
                    for ht in range(NHT):
                        sc.wait_ge(S["s_gpe"], ht + 1)
                        if ht >= 2:
                            sc.wait_ge(S["s_gt"], ht - 1)
                        sc.activation(
                            sg[:, ht % 2, :],
                            ps[:, (ht % 2) * 1024:(ht % 2 + 1) * 1024],
                            AF.Silu,
                        ).then_inc(S["s_silu"], 1)
                    sc.wait_ge(S["s_gtw0"], 16 * 22)
                    sc.wait_ge(S["s_gtw1"], 16 * 21)
                    sc.wait_ge(S["s_gtw2"], 16 * 21)
                    sc.wait_ge(S["s_wdw0"], 16 * NHT)
                    for p4 in range(4):
                        for hc in range(NHT):
                            gidx2 = 64 * p4 + hc
                            if gidx2 >= 3:
                                sc.wait_ge(S["s_dhc"], gidx2 - 2)
                            sc.dma_start(
                                out=wdd[:, gidx2 % 3, :],
                                in_=wdb.ap()[hc * 128:(hc + 1) * 128, :],
                            ).then_inc(S[f"s_wdd{gidx2 % 3}"], 16)
                        for tt in (2 * p4, 2 * p4 + 1):
                            sc.wait_ge(S["s_var2"], tt + 1)
                            sc.activation(lnv[:, :], var_[:, :], AF.Ln, bias=epsv[:, :])
                            sc.drain()
                            sc.activation(rstd[:, :], lnv[:, :], AF.Exp, scale=-0.5).then_inc(S["s_actr2"], 1)
                            sc.drain()
                            sc.wait_ge(S["s_ovr"], tt + 1)
                            sc.dma_start(
                                out=out[tt * 128:(tt + 1) * 128, :],
                                in_=outst[:, tt % 2, :],
                            ).then_inc(S[f"s_out{tt % 2}"], 16)

                @b3.vector
                def _(v):
                    for ht in range(NHT):
                        v.wait_ge(S["s_silu"], ht + 1)
                        v.wait_ge(S["s_upe"], ht + 1)
                        if ht >= 3:
                            v.wait_ge(S[f"s_gtw{ht % 3}"], 16 * (ht // 3))
                        v.tensor_mul(
                            gtw[:, ht % 3, :],
                            sg[:, ht % 2, :],
                            ps[:, (4 + 2 * (ht % 2)) * 512:(6 + 2 * (ht % 2)) * 512],
                        ).then_inc(S["s_gt"], 1)
                    for p4 in range(4):
                        v.wait_ge(S["s_dhc"], 64 * (p4 + 1))
                        for tt in (2 * p4, 2 * p4 + 1):
                            v.wait_ge(S[f"s_h1ld{tt % 2}"], 16 * (tt // 2 + 1))
                            v.scalar_tensor_tensor(
                                z2[:, tt % 2, :], ps[:, (tt % 2) * 2048:(tt % 2 + 1) * 2048], 1.0,
                                h1t_sb[:, tt % 2, :], ALU.mult, ALU.add,
                            ).then_inc(S["s_y2free"], 1)
                            v.drain()
                        for tt in (2 * p4, 2 * p4 + 1):
                            v.reduce_sum(rs[:, :], z2[:, tt % 2, :], axis=AX.X)
                            v.drain()
                            v.tensor_scalar_mul(negmean[:, :], rs[:, :], -1.0 / D)
                            v.drain()
                            v.tensor_scalar_add(z2[:, tt % 2, :], z2[:, tt % 2, :], negmean[:, :])
                            v.drain()
                            if tt >= 2:
                                v.wait_ge(S[f"s_out{tt % 2}"], 16 * (tt // 2))
                            v.scalar_tensor_tensor(
                                outst[:, tt % 2, :], z2[:, tt % 2, :], 1.0, z2[:, tt % 2, :],
                                ALU.mult, ALU.mult, accum_out=ssq[:, :],
                            )
                            v.drain()
                            v.tensor_scalar_mul(var_[:, :], ssq[:, :], 1.0 / D).then_inc(S["s_var2"], 1)
                            v.wait_ge(S["s_actr2"], tt + 1)
                            v.tensor_scalar_mul(outst[:, tt % 2, :], z2[:, tt % 2, :], rstd[:, :]).then_inc(S["s_ovr"], 1)
                            v.drain()

                @b3.sync
                def _(sy):
                    for ht in range(NHT):
                        sy.wait_ge(S["s_gt"], ht + 1)
                        sy.dma_start(
                            out=gtd.ap()[ht * 128:(ht + 1) * 128, :],
                            in_=gtw[:, ht % 3, :],
                        ).then_inc(S[f"s_gtw{ht % 3}"], 16)
                    sy.wait_ge(S["s_wope"], NTT)
                    sy.wait_ge(S["s_gtw0"], 16 * 22)
                    sy.wait_ge(S["s_gtw1"], 16 * 21)
                    sy.wait_ge(S["s_gtw2"], 16 * 21)
                    sy.wait_ge(S["s_wdw0"], 16 * NHT)
                    sy.wait_ge(S["s_h1d0"], 16 * (NTT // 2))
                    sy.wait_ge(S["s_h1d1"], 16 * (NTT // 2))
                    gtd_r = gtd.ap().rearrange("(hc k) t -> k hc t", k=128)
                    for p4 in range(4):
                        if p4 >= 2:
                            sy.wait_ge(S["s_dhc"], 64 * (p4 - 1))
                        for oc in range(8):
                            sy.dma_start(
                                out=gtp_view[:, p4 % 2, 8 * oc:8 * (oc + 1), :],
                                in_=gtd_r[:, 8 * oc:8 * (oc + 1), 256 * p4:256 * (p4 + 1)],
                            ).then_inc(S[f"s_gtr{p4 % 2}"], 16)
                        for tt in (2 * p4, 2 * p4 + 1):
                            if tt >= 2:
                                sy.wait_ge(S["s_y2free"], tt - 1)
                            sy.dma_start(
                                out=h1t_sb[:, tt % 2, :],
                                in_=h1d.ap()[tt * 128:(tt + 1) * 128, :],
                            ).then_inc(S[f"s_h1ld{tt % 2}"], 16)
                    sy.wait_ge(S["s_out0"], 16 * (NTT // 2))
                    sy.wait_ge(S["s_out1"], 16 * (NTT // 2))

    return nc


_NC_CACHE = []


def kernel(L0_attn_x, L0_attn_Wq, L0_attn_Wo, L0_attn_bo,
           L0_ffn_w_gate, L0_ffn_w_up, L0_ffn_w_down):
    x = np.asarray(L0_attn_x, dtype=np.float32)
    Wq = np.asarray(L0_attn_Wq, dtype=np.float32)
    Wo = np.ascontiguousarray(np.asarray(L0_attn_Wo, dtype=np.float32))
    bov = np.ascontiguousarray(np.asarray(L0_attn_bo, dtype=np.float32))
    wgt = np.ascontiguousarray(
        np.asarray(L0_ffn_w_gate, dtype=np.float32).reshape(NHT, 128, NDC, 128).transpose(0, 3, 2, 1))
    wut = np.ascontiguousarray(
        np.asarray(L0_ffn_w_up, dtype=np.float32).reshape(NHT, 128, NDC, 128).transpose(0, 3, 2, 1))
    wdT = np.ascontiguousarray(np.asarray(L0_ffn_w_down, dtype=np.float32).T)
    eye = np.eye(128, dtype=np.float32)
    xT = [np.ascontiguousarray(x[b].T) for b in range(2)]

    in_maps = []
    for c in range(8):
        b, r4 = c // 4, c % 4
        wqt = np.ascontiguousarray(
            Wq[:, r4 * 512:(r4 + 1) * 512].reshape(NDC, 128, NHC, 128).transpose(1, 0, 2, 3))
        in_maps.append(dict(
            xt=xT[b],
            xres=np.ascontiguousarray(x[b, r4 * MINE:(r4 + 1) * MINE]),
            wq=wqt,
            wo=Wo, bo=bov, wg=wgt, wu=wut, wd=wdT, eye=eye,
        ))

    if not _NC_CACHE:
        _NC_CACHE.append(build())
    nc = _NC_CACHE[0]
    res = run_bass_kernel_spmd(nc, in_maps, core_ids=list(range(8)))
    pieces = [res.results[c]["out"] for c in range(8)]
    return np.concatenate(pieces, axis=0).reshape(2, N, D)


if __name__ == "__main__":
    rng = np.random.default_rng(1)
    x = rng.standard_normal((2, N, D), dtype=np.float32)
    Wq = (rng.standard_normal((D, D), dtype=np.float32) * D ** -0.5).astype(np.float32)
    Wo = (rng.standard_normal((D, D), dtype=np.float32) * D ** -0.5).astype(np.float32)
    bov = rng.standard_normal(D).astype(np.float32) * 0.1
    wgm = (rng.uniform(-1, 1, (HID, D)) / np.sqrt(D)).astype(np.float32)
    wum = (rng.uniform(-1, 1, (HID, D)) / np.sqrt(D)).astype(np.float32)
    wdm = (rng.uniform(-1, 1, (D, HID)) / np.sqrt(HID)).astype(np.float32)

    def ref():
        def ln(a):
            m = a.mean(-1, keepdims=True)
            v = a.var(-1, keepdims=True)
            return (a - m) / np.sqrt(v + EPS)
        q = x @ Wq
        qh = q.reshape(2, N, 16, HD).transpose(0, 2, 1, 3)
        s = np.einsum("bhnd,bhmd->bhnm", qh, qh) * HD ** -0.5
        s = s - s.max(-1, keepdims=True)
        a = np.exp(s)
        a /= a.sum(-1, keepdims=True)
        y = np.einsum("bhnm,bhmd->bhnd", a, qh).transpose(0, 2, 1, 3).reshape(2, N, D)
        h = ln(x + y @ Wo + bov)
        gate = h @ wgm.T
        up = h @ wum.T
        gated = gate / (1 + np.exp(-gate)) * up
        return ln(h + gated @ wdm.T)

    got = kernel(L0_attn_x=x, L0_attn_Wq=Wq, L0_attn_Wo=Wo, L0_attn_bo=bov,
                 L0_ffn_w_gate=wgm, L0_ffn_w_up=wum, L0_ffn_w_down=wdm)
    want = ref()
    err = np.sqrt(np.mean((got - want) ** 2) / np.mean(want ** 2))
    print("rel l2:", err)
    print("max abs:", np.abs(got - want).max())


# version 06
# speedup vs baseline: 1.0907x; 1.0223x over previous
import numpy as np
import concourse.bass as bass
import concourse.mybir as mybir
from concourse.bass_utils import run_bass_kernel_spmd

F32 = mybir.dt.float32
BF16 = mybir.dt.bfloat16
AF = mybir.ActivationFunctionType
ALU = mybir.AluOpType
AX = mybir.AxisListType

N = 4096
D = 2048
HD = 128
NHC = 4
MINE = 1024
HID = 8192
NDC = D // 128
NHT = HID // 128
NKT = N // 128
NTT = MINE // 128
ISCALE = float(1.0 / np.sqrt(HD))
EPS = 1e-5
GROUPS = [[0, 1, 2, 3], [4, 5, 6, 7]]


def build():
    nc = bass.Bass("TRN2", target_bir_lowering=False)

    xt = nc.dram_tensor("xt", [D, N], F32, kind="ExternalInput").ap()
    xres = nc.dram_tensor("xres", [MINE, D], F32, kind="ExternalInput").ap()
    wq = nc.dram_tensor("wq", [128, NDC, NHC, HD], F32, kind="ExternalInput").ap()
    wo = nc.dram_tensor("wo", [D, D], F32, kind="ExternalInput").ap()
    bo = nc.dram_tensor("bo", [D], F32, kind="ExternalInput").ap()
    wg = nc.dram_tensor("wg", [NHT, 128, NDC, 128], F32, kind="ExternalInput").ap()
    wu = nc.dram_tensor("wu", [NHT, 128, NDC, 128], F32, kind="ExternalInput").ap()
    wd = nc.dram_tensor("wd", [HID, D], F32, kind="ExternalInput").ap()
    eye = nc.dram_tensor("eye", [128, 128], F32, kind="ExternalInput").ap()
    out = nc.dram_tensor("out", [MINE, D], F32, kind="ExternalOutput").ap()

    ag_send = [nc.dram_tensor(f"ag_send{h}", [HD, N], BF16) for h in range(NHC)]
    ag_recv = [nc.dram_tensor(f"ag_recv{h}", [4 * HD, N], BF16) for h in range(NHC)]
    zdr = nc.dram_tensor("zdr", [NHC, N], F32)
    h1d = nc.dram_tensor("h1d", [MINE, D], F32)
    gtd = nc.dram_tensor("gtd", [HID, MINE], BF16)
    wdb = nc.dram_tensor("wdb", [HID, D], BF16)

    sem_names = [
        "s_pre", "s_xt0", "s_xt1", "s_xt2", "s_xt3",
        "s_qev", "s_peq", "s_vt", "s_vte",
        "s_sS", "s_exp", "s_zac", "s_av", "s_otev", "s_zrow", "s_zw", "s_zbc", "s_otA", "s_otn",
        "s_ag", "s_cc",
        "s_wo", "s_otg0", "s_otg1", "s_otg2", "s_otg3", "s_xr0", "s_xr1", "s_bo", "s_wope", "s_ypsfree",
        "s_var1", "s_actr", "s_h1f", "s_h1d0", "s_h1d1", "s_tp", "s_tpe",
        "s_wg0", "s_wg1", "s_wg2", "s_gpe", "s_upe", "s_silu", "s_gt",
        "s_gtw0", "s_gtw1", "s_gtw2",
        "s_wdr0", "s_wdr1", "s_wdr2", "s_wdr3",
        "s_wdw0", "s_wdw1", "s_wdw2", "s_wdw3",
        "s_gtr0", "s_gtr1", "s_wdd0", "s_wdd1", "s_wdd2", "s_wdd3", "s_dhc", "s_y2free",
        "s_var2", "s_actr2", "s_h1ld0", "s_h1ld1", "s_ovr", "s_out0", "s_out1",
    ]

    import contextlib
    with contextlib.ExitStack() as est:
        S = {n: est.enter_context(nc.semaphore(n)) for n in sem_names}
        ps = est.enter_context(nc.psum_tensor("ps", [128, 4096], F32))

        def bank(i, n=1):
            return ps[:, 512 * i: 512 * (i + n)]

        def bank_bf(i):
            return ps[:, 512 * i: 512 * i + 64].bitcast(BF16)

        def bank32(i):
            return ps[:, 512 * i: 512 * i + 128]

        eye_bf = est.enter_context(nc.sbuf_tensor("eye_bf", [128, 128], BF16))
        eye32 = est.enter_context(nc.sbuf_tensor("eye32", [128, 128], F32))
        big64 = est.enter_context(nc.sbuf_tensor("big64", [128, 2 * NHT * 256], BF16))
        wo_view = big64[:, :].rearrange("p (dc n) -> p dc n", dc=NDC)
        gtp_view = big64[:, :].rearrange("p (b hc m) -> p b hc m", b=2, hc=NHT)
        rs = est.enter_context(nc.sbuf_tensor("rs", [128, 1], F32))
        negmean = est.enter_context(nc.sbuf_tensor("negmean", [128, 1], F32))
        ssq = est.enter_context(nc.sbuf_tensor("ssq", [128, 1], F32))
        var_ = est.enter_context(nc.sbuf_tensor("var_", [128, 1], F32))
        lnv = est.enter_context(nc.sbuf_tensor("lnv", [128, 1], F32))
        rstd = est.enter_context(nc.sbuf_tensor("rstd", [128, 1], F32))
        epsv = est.enter_context(nc.sbuf_tensor("epsv", [128, 1], F32))
        wdstage = est.enter_context(nc.sbuf_tensor("wdstage", [128, 1, D], BF16))

        regs = {}

        with (
            nc.sbuf_tensor("qT", [128, NHC, N], BF16) as qT,
            nc.sbuf_tensor("v_sb", [128, NHC, NKT, HD], BF16) as v_sb,
            nc.sbuf_tensor("xt_sb", [128, 3, 1024], BF16) as xt_sb,
            nc.sbuf_tensor("wq_sb", [128, NDC, NHC, HD], BF16) as wq_sb,
            nc.sbuf_tensor("P_sb", [128, 2, 1024], BF16) as P_sb,
            nc.sbuf_tensor("ot_raw", [128, 2, N], F32) as ot_raw,
            nc.sbuf_tensor("zp", [128, NHC, 4, NKT], F32) as zp,
            nc.sbuf_tensor("zs", [128, NKT], F32) as zs,
            nc.sbuf_tensor("zinv_s", [128, NKT], F32) as zinv_s,
            nc.sbuf_tensor("zrowt", [32, 128], F32) as zrowt,
            nc.sbuf_tensor("zinvb", [128, N // 2], F32) as zinvb,
            nc.sbuf_tensor("otn", [128, N // 2], BF16) as otn,
            nc.Block() as blk,
        ):
            @blk.gpsimd
            def _(g):
                g.dma_start(out=wq_sb[:, :, :, :].rearrange("p dc h c -> p (dc h c)"),
                            in_=wq.rearrange("p dc h c -> p (dc h c)")).then_inc(S["s_pre"], 16)
                g.dma_start(out=eye_bf[:, :], in_=eye).then_inc(S["s_pre"], 16)
                for s4 in range(4):
                    for dc in range(NDC):
                        j = 16 * s4 + dc
                        if j >= 3:
                            g.wait_ge(S["s_peq"], j - 2)
                        g.dma_start(
                            out=xt_sb[:, j % 3, :],
                            in_=xt[dc * 128:(dc + 1) * 128, s4 * 1024:(s4 + 1) * 1024],
                        ).then_inc(S[f"s_xt{j % 3}"], 16)
                for dq in range(4):
                    g.dma_start(
                        out=wo_view[:, dq * 4:(dq + 1) * 4, :],
                        in_=wo[dq * 512:(dq + 1) * 512, :].rearrange("(dc p) n -> p dc n", p=128),
                    ).then_inc(S["s_wo"], 16)
                def wd_job(lo, hi):
                    for ht2 in range(lo, hi):
                        if ht2 >= 1:
                            g.wait_ge(S["s_wdw0"], 16 * ht2)
                        g.dma_start(
                            out=wdstage[:, 0, :],
                            in_=wd[ht2 * 128:(ht2 + 1) * 128, :],
                        ).then_inc(S["s_wdr0"], 16)
                        g.wait_ge(S["s_wdr0"], 16 * (ht2 + 1))
                        g.dma_start(
                            out=wdb.ap()[ht2 * 128:(ht2 + 1) * 128, :],
                            in_=wdstage[:, 0, :],
                        ).then_inc(S["s_wdw0"], 16)

                for h in range(NHC):
                    if h < 3:
                        wd_job(16 * h, 16 * (h + 1))
                    g.wait_ge(S["s_otA"], h + 1)
                    g.dma_start(out=ag_send[h].ap()[:, 0:N // 2], in_=otn[:, :]).then_inc(S["s_ag"], 16)
                    g.wait_ge(S["s_otn"], h + 1)
                    g.dma_start(out=ag_send[h].ap()[:, N // 2:N], in_=otn[:, :]).then_inc(S["s_ag"], 16)
                    g.wait_ge(S["s_ag"], 32 * (h + 1))
                    g.collective_compute(
                        "AllGather", ALU.bypass, replica_groups=GROUPS,
                        ins=[ag_send[h].ap()], outs=[ag_recv[h].ap()],
                    ).then_inc(S["s_cc"], 1)
                wd_job(48, 64)


            @blk.tensor
            def _(t):
                t.wait_ge(S["s_pre"], 32)
                for s4 in range(4):
                    for dc in range(NDC):
                        j = 16 * s4 + dc
                        t.wait_ge(S[f"s_xt{j % 3}"], 16 * (j // 3 + 1))
                        if dc == 0 and s4 > 0:
                            t.wait_ge(S["s_qev"], 4 * s4)
                        ins = None
                        for h in range(NHC):
                            for half in range(2):
                                ins = t.matmul(
                                    bank(2 * h + half),
                                    wq_sb[:, dc, h, :],
                                    xt_sb[:, j % 3, 512 * half:512 * (half + 1)],
                                    start=(dc == 0), stop=(dc == NDC - 1),
                                )
                        ins.then_inc(S["s_peq"], 1)
                for h in range(NHC):
                    for kt in range(NKT):
                        j = NKT * h + kt
                        if j == 0:
                            t.wait_ge(S["s_qev"], 16)
                        if j >= 2:
                            t.wait_ge(S["s_vte"], j - 1)
                        t.transpose(
                            bank_bf(j % 2),
                            qT[:, h, kt * 128:(kt + 1) * 128],
                            eye_bf[:, :],
                        ).then_inc(S["s_vt"], 1)
                for gidx in range(16):
                    h, qq = gidx // 4, gidx % 4
                    po = gidx % 2

                    def emit_S(kt, gidx=gidx, h=h, qq=qq):
                        idx = 32 * gidx + kt
                        if idx == 0:
                            t.wait_ge(S["s_vte"], 128)
                        par = kt % 2
                        ins = None
                        for half in range(2):
                            ins = t.matmul(
                                bank(2 * par + half),
                                qT[:, h, kt * 128:(kt + 1) * 128],
                                qT[:, h, qq * 1024 + 512 * half: qq * 1024 + 512 * (half + 1)],
                                start=True, stop=True,
                            )
                        ins.then_inc(S["s_sS"], 1)

                    def emit_AV(kt, gidx=gidx, h=h, po=po):
                        idx = 32 * gidx + kt
                        t.wait_ge(S["s_exp"], idx + 1)
                        if kt == 0 and gidx >= 2:
                            t.wait_ge(S["s_otev"], gidx - 1)
                        ins = None
                        for half in range(2):
                            ins = t.matmul(
                                bank(4 + 2 * po + half),
                                v_sb[:, h, kt, :],
                                P_sb[:, kt % 2, 512 * half:512 * (half + 1)],
                                start=(kt == 0), stop=(kt == NKT - 1),
                            )
                        ins.then_inc(S["s_av"], 1)

                    emit_S(0)
                    emit_S(1)
                    for kt in range(NKT):
                        emit_AV(kt)
                        if kt + 2 < NKT:
                            emit_S(kt + 2)

            @blk.scalar
            def _(sc):
                for gidx in range(16):
                    h, qq = gidx // 4, gidx % 4
                    for kt in range(NKT):
                        idx = 32 * gidx + kt
                        par = kt % 2
                        sc.wait_ge(S["s_sS"], idx + 1)
                        if idx >= 2:
                            sc.wait_ge(S["s_av"], idx - 1)
                        sc.activation(
                            P_sb[:, par, :],
                            ps[:, par * 1024:(par + 1) * 1024],
                            AF.Exp, scale=ISCALE,
                            accum_out=zp[:, h, qq, kt:kt + 1].opt(),
                        ).then_inc(S["s_exp"], 1)

            @blk.vector
            def _(v):
                v.memset(epsv[:, :], EPS)
                for s4 in range(4):
                    v.wait_ge(S["s_peq"], 16 * (s4 + 1))
                    for h in range(NHC):
                        v.tensor_copy(
                            qT[:, h, s4 * 1024:(s4 + 1) * 1024], bank(2 * h, 2)
                        ).then_inc(S["s_qev"], 1)
                for h in range(NHC):
                    for kt in range(NKT):
                        j = NKT * h + kt
                        v.wait_ge(S["s_vt"], j + 1)
                        v.tensor_copy(v_sb[:, h, kt, :], bank_bf(j % 2)).then_inc(S["s_vte"], 1)

                def emit_otn(hp):
                    v.wait_ge(S["s_zbc"], 32 * hp + 16)
                    if hp >= 1:
                        v.wait_ge(S["s_ag"], 32 * hp)
                    ins = None
                    for q2 in range(4):
                        if q2 == 2:
                            ins.then_inc(S["s_otA"], 1)
                            v.wait_ge(S["s_zbc"], 32 * hp + 32)
                            v.wait_ge(S["s_ag"], 32 * hp + 16)
                        elif q2 > 0:
                            v.drain()
                        ins = v.tensor_mul(
                            otn[:, (q2 % 2) * 1024:(q2 % 2 + 1) * 1024],
                            ot_raw[:, hp % 2, q2 * 1024:(q2 + 1) * 1024],
                            zinvb[:, (q2 % 2) * 1024:(q2 % 2 + 1) * 1024],
                        )
                    ins.then_inc(S["s_otn"], 1)

                for gidx in range(16):
                    h, qq = gidx // 4, gidx % 4
                    po = gidx % 2
                    v.wait_ge(S["s_av"], 32 * (gidx + 1))
                    v.tensor_copy(
                        ot_raw[:, h % 2, qq * 1024:(qq + 1) * 1024], bank(4 + 2 * po, 2)
                    ).then_inc(S["s_otev"], 1)
                    if qq == 3:
                        v.tensor_add(zs[:, :], zp[:, h, 0, :], zp[:, h, 1, :])
                        v.drain()
                        v.tensor_add(zs[:, :], zs[:, :], zp[:, h, 2, :])
                        v.drain()
                        v.tensor_add(zs[:, :], zs[:, :], zp[:, h, 3, :])
                        v.drain()
                        v.reciprocal(zinv_s[:, :], zs[:, :])
                        v.drain()
                        ins = None
                        for pb in range(4):
                            if pb > 0:
                                v.drain()
                            ins = v.transpose(
                                zrowt[0:32, 32 * pb:32 * (pb + 1)],
                                zinv_s[32 * pb:32 * (pb + 1), 0:32],
                            )
                        ins.then_inc(S["s_zrow"], 1)
                    if qq == 0 and gidx >= 4:
                        emit_otn(gidx // 4 - 1)
                emit_otn(3)

            @blk.sync
            def _(sy):
                pid = sy.partition_id()
                regs["r4"] = sy.compute_val(pid % 4)
                sy.dma_start(out=eye32[:, :], in_=eye).then_inc(S["s_bo"], 16)
                for h in range(NHC):
                    sy.wait_ge(S["s_zrow"], h + 1)
                    sy.dma_start(
                        out=zdr.ap()[h].rearrange("(kt p) -> kt p", p=128),
                        in_=zrowt[0:32, :],
                    ).then_inc(S["s_zw"], 16)
                    sy.wait_ge(S["s_zw"], 16 * (h + 1))
                    if h >= 1:
                        sy.wait_ge(S["s_otn"], h)
                    sy.dma_start(
                        out=zinvb[:, :],
                        in_=zdr.ap()[h][0:N // 2].partition_broadcast(128),
                    ).then_inc(S["s_zbc"], 16)
                    sy.wait_ge(S["s_otA"], h + 1)
                    sy.dma_start(
                        out=zinvb[:, :],
                        in_=zdr.ap()[h][N // 2:N].partition_broadcast(128),
                    ).then_inc(S["s_zbc"], 16)

        with (
            nc.sbuf_tensor("wg_sb", [128, 3, NDC, 128], BF16) as wg_sb,
            nc.sbuf_tensor("wu_sb", [128, 3, NDC, 128], BF16) as wu_sb,
            nc.sbuf_tensor("h1T", [128, NDC, MINE], BF16) as h1T,
        ):
            with (
                nc.sbuf_tensor("otg", [128, NDC, MINE], BF16) as otg,
                nc.sbuf_tensor("xres_sb", [128, 2, D], F32) as xres_sb,
                nc.sbuf_tensor("bo_bc", [128, D], F32) as bo_bc,
                nc.sbuf_tensor("z1", [128, D], F32) as z1,
                nc.sbuf_tensor("h1f", [128, 2, D], F32) as h1f,
                nc.Block() as b2,
            ):
                @b2.sync
                def _(sy):
                    def otg_load(hl):
                        for r in range(4):
                            sy.dma_start(
                                out=otg[:, 4 * r + hl, :],
                                in_=ag_recv[hl].ap()[r * 128:(r + 1) * 128, bass.ts(regs["r4"], MINE)],
                            ).then_inc(S[f"s_otg{hl}"], 16)
                    for hl in range(3):
                        sy.wait_ge(S["s_cc"], hl + 1)
                        otg_load(hl)
                    sy.dma_start(out=bo_bc[:, :], in_=bo.partition_broadcast(128)).then_inc(S["s_bo"], 16)
                    sy.wait_ge(S["s_cc"], NHC)
                    otg_load(3)
                    for tt in range(NTT):
                        if tt >= 2:
                            sy.wait_ge(S["s_h1f"], tt - 1)
                        sy.dma_start(
                            out=xres_sb[:, tt % 2, :],
                            in_=xres[tt * 128:(tt + 1) * 128, :],
                        ).then_inc(S[f"s_xr{tt % 2}"], 16)

                @b2.tensor
                def _(t):
                    t.wait_ge(S["s_exp"], 512)
                    t.wait_ge(S["s_otev"], 16)

                    def emit_tr(tt_src):
                        t.wait_ge(S["s_h1f"], tt_src + 1)
                        for j16 in range(16):
                            jj = 16 * tt_src + j16
                            if jj == 0:
                                pass
                            if jj >= 2:
                                t.wait_ge(S["s_tpe"], jj - 1)
                            t.transpose(
                                bank32(4 + jj % 2),
                                h1f[:, tt_src % 2, j16 * 128:(j16 + 1) * 128],
                                eye32[:, :],
                            ).then_inc(S["s_tp"], 1)

                    dc_order = [d for d in range(NDC) if d % 4 != 3] + [d for d in range(NDC) if d % 4 == 3]
                    for tt in range(NTT):
                        if tt == 0:
                            t.wait_ge(S["s_wo"], 16 * 4)
                        if tt >= 1:
                            t.wait_ge(S["s_ypsfree"], tt)
                        ins = None
                        for di, dc in enumerate(dc_order):
                            if tt == 0:
                                t.wait_ge(S[f"s_otg{dc % 4}"], 16 * 4)
                            for half in range(4):
                                ins = t.matmul(
                                    bank(half),
                                    otg[:, dc, tt * 128:(tt + 1) * 128],
                                    wo_view[:, dc, 512 * half:512 * (half + 1)],
                                    start=(di == 0), stop=(di == NDC - 1),
                                )
                        ins.then_inc(S["s_wope"], 1)
                        if tt >= 1:
                            emit_tr(tt - 1)
                    emit_tr(NTT - 1)

                @b2.vector
                def _(v):
                    def emit_trev(tt_src):
                        for j16 in range(16):
                            jj = 16 * tt_src + j16
                            v.wait_ge(S["s_tp"], jj + 1)
                            v.tensor_copy(
                                h1T[:, j16, tt_src * 128:(tt_src + 1) * 128],
                                bank32(4 + jj % 2),
                            ).then_inc(S["s_tpe"], 1)

                    for tt in range(NTT):
                        v.wait_ge(S["s_wope"], tt + 1)
                        if tt == 0:
                            v.wait_ge(S["s_bo"], 32)
                        v.scalar_tensor_tensor(
                            z1[:, :], ps[:, 0:2048], 1.0, bo_bc[:, :], ALU.mult, ALU.add
                        ).then_inc(S["s_ypsfree"], 1)
                        v.drain()
                        v.wait_ge(S[f"s_xr{tt % 2}"], 16 * (tt // 2 + 1))
                        v.tensor_add(z1[:, :], z1[:, :], xres_sb[:, tt % 2, :])
                        v.drain()
                        v.reduce_sum(rs[:, :], z1[:, :], axis=AX.X)
                        v.drain()
                        v.tensor_scalar_mul(negmean[:, :], rs[:, :], -1.0 / D)
                        v.drain()
                        v.tensor_scalar_add(z1[:, :], z1[:, :], negmean[:, :])
                        v.drain()
                        if tt >= 2:
                            v.wait_ge(S[f"s_h1d{tt % 2}"], 16 * (tt // 2))
                            v.wait_ge(S["s_tp"], 16 * (tt - 1))
                        v.scalar_tensor_tensor(
                            h1f[:, tt % 2, :], z1[:, :], 1.0, z1[:, :], ALU.mult, ALU.mult,
                            accum_out=ssq[:, :],
                        )
                        v.drain()
                        v.tensor_scalar_mul(var_[:, :], ssq[:, :], 1.0 / D).then_inc(S["s_var1"], 1)
                        v.wait_ge(S["s_actr"], tt + 1)
                        v.tensor_scalar_mul(h1f[:, tt % 2, :], z1[:, :], rstd[:, :]).then_inc(S["s_h1f"], 1)
                        v.drain()
                        if tt >= 1:
                            emit_trev(tt - 1)
                    emit_trev(NTT - 1)

                @b2.scalar
                def _(sc):
                    for tt in range(NTT):
                        sc.wait_ge(S["s_var1"], tt + 1)
                        sc.activation(lnv[:, :], var_[:, :], AF.Ln, bias=epsv[:, :])
                        sc.drain()
                        sc.activation(rstd[:, :], lnv[:, :], AF.Exp, scale=-0.5).then_inc(S["s_actr"], 1)
                        sc.drain()
                        sc.wait_ge(S["s_h1f"], tt + 1)
                        sc.dma_start(
                            out=h1d.ap()[tt * 128:(tt + 1) * 128, :],
                            in_=h1f[:, tt % 2, :],
                        ).then_inc(S[f"s_h1d{tt % 2}"], 16)

            with (
                nc.sbuf_tensor("sg", [128, 2, 1024], F32) as sg,
                nc.sbuf_tensor("gtw", [128, 3, 1024], BF16) as gtw,
                nc.sbuf_tensor("wdd", [128, 4, D], BF16) as wdd,
                nc.sbuf_tensor("h1t_sb", [128, 2, D], F32) as h1t_sb,
                nc.sbuf_tensor("outst", [128, 2, D], F32) as outst,
                nc.sbuf_tensor("z2", [128, 2, D], F32) as z2,
                nc.Block() as b3,
            ):
                @b3.gpsimd
                def _(g):
                    def load_wgu(ht):
                        for w_ap, w_sb in ((wg, wg_sb), (wu, wu_sb)):
                            g.dma_start(
                                out=w_sb[:, ht % 3, :, :].rearrange("p dc c -> p (dc c)"),
                                in_=w_ap[ht].rearrange("p dc c -> p (dc c)"),
                            ).then_inc(S[f"s_wg{ht % 3}"], 16)

                    load_wgu(0)
                    load_wgu(1)
                    load_wgu(2)
                    for ht in range(NHT):
                        if ht + 3 < NHT:
                            g.wait_ge(S["s_upe"], ht + 1)
                            load_wgu(ht + 3)

                @b3.tensor
                def _(t):
                    for ht in range(NHT):
                        t.wait_ge(S[f"s_wg{ht % 3}"], 32 * (ht // 3 + 1))
                        if ht == 0:
                            t.wait_ge(S["s_tpe"], 128)
                        if ht >= 2:
                            t.wait_ge(S["s_silu"], ht - 1)
                        ins = None
                        for dc in range(NDC):
                            for half in range(2):
                                ins = t.matmul(
                                    bank(2 * (ht % 2) + half),
                                    wg_sb[:, ht % 3, dc, :],
                                    h1T[:, dc, 512 * half:512 * (half + 1)],
                                    start=(dc == 0), stop=(dc == NDC - 1),
                                )
                        ins.then_inc(S["s_gpe"], 1)
                        if ht >= 2:
                            t.wait_ge(S["s_gt"], ht - 1)
                        ins = None
                        for dc in range(NDC):
                            for half in range(2):
                                ins = t.matmul(
                                    bank(4 + 2 * (ht % 2) + half),
                                    wu_sb[:, ht % 3, dc, :],
                                    h1T[:, dc, 512 * half:512 * (half + 1)],
                                    start=(dc == 0), stop=(dc == NDC - 1),
                                )
                        ins.then_inc(S["s_upe"], 1)
                    t.wait_ge(S["s_silu"], NHT)
                    t.wait_ge(S["s_gt"], NHT)
                    for p4 in range(4):
                        t.wait_ge(S[f"s_gtr{p4 % 2}"], 128 * (p4 // 2 + 1))
                        if p4 >= 1:
                            t.wait_ge(S["s_y2free"], 2 * p4)
                        for hc in range(NHT):
                            gidx2 = 64 * p4 + hc
                            t.wait_ge(S[f"s_wdd{gidx2 % 4}"], 16 * (gidx2 // 4 + 1))
                            ins = None
                            for side in range(2):
                                for half in range(4):
                                    ins = t.matmul(
                                        bank(4 * side + half),
                                        gtp_view[:, p4 % 2, hc, 128 * side:128 * (side + 1)],
                                        wdd[:, gidx2 % 4, 512 * half:512 * (half + 1)],
                                        start=(hc == 0), stop=(hc == NHT - 1),
                                    )
                            ins.then_inc(S["s_dhc"], 1)

                @b3.scalar
                def _(sc):
                    for ht in range(NHT):
                        sc.wait_ge(S["s_gpe"], ht + 1)
                        if ht >= 2:
                            sc.wait_ge(S["s_gt"], ht - 1)
                        sc.activation(
                            sg[:, ht % 2, :],
                            ps[:, (ht % 2) * 1024:(ht % 2 + 1) * 1024],
                            AF.Silu,
                        ).then_inc(S["s_silu"], 1)
                    sc.wait_ge(S["s_gtw0"], 16 * 22)
                    sc.wait_ge(S["s_gtw1"], 16 * 21)
                    sc.wait_ge(S["s_gtw2"], 16 * 21)
                    sc.wait_ge(S["s_wdw0"], 16 * NHT)
                    sc.wait_ge(S["s_h1d0"], 16 * (NTT // 2))
                    sc.wait_ge(S["s_h1d1"], 16 * (NTT // 2))
                    for p4 in range(4):
                        for tt in (2 * p4, 2 * p4 + 1):
                            if tt >= 2:
                                sc.wait_ge(S["s_y2free"], tt - 1)
                            sc.dma_start(
                                out=h1t_sb[:, tt % 2, :],
                                in_=h1d.ap()[tt * 128:(tt + 1) * 128, :],
                            ).then_inc(S[f"s_h1ld{tt % 2}"], 16)
                        for hc in range(NHT):
                            gidx2 = 64 * p4 + hc
                            if gidx2 >= 4:
                                sc.wait_ge(S["s_dhc"], gidx2 - 3)
                            sc.dma_start(
                                out=wdd[:, gidx2 % 4, :],
                                in_=wdb.ap()[hc * 128:(hc + 1) * 128, :],
                            ).then_inc(S[f"s_wdd{gidx2 % 4}"], 16)
                        for tt in (2 * p4, 2 * p4 + 1):
                            sc.wait_ge(S["s_var2"], tt + 1)
                            sc.activation(lnv[:, :], var_[:, :], AF.Ln, bias=epsv[:, :])
                            sc.drain()
                            sc.activation(rstd[:, :], lnv[:, :], AF.Exp, scale=-0.5).then_inc(S["s_actr2"], 1)
                            sc.drain()
                            sc.wait_ge(S["s_ovr"], tt + 1)
                            sc.dma_start(
                                out=out[tt * 128:(tt + 1) * 128, :],
                                in_=outst[:, tt % 2, :],
                            ).then_inc(S[f"s_out{tt % 2}"], 16)

                @b3.vector
                def _(v):
                    for ht in range(NHT):
                        v.wait_ge(S["s_silu"], ht + 1)
                        v.wait_ge(S["s_upe"], ht + 1)
                        if ht >= 3:
                            v.wait_ge(S[f"s_gtw{ht % 3}"], 16 * (ht // 3))
                        v.tensor_mul(
                            gtw[:, ht % 3, :],
                            sg[:, ht % 2, :],
                            ps[:, (4 + 2 * (ht % 2)) * 512:(6 + 2 * (ht % 2)) * 512],
                        ).then_inc(S["s_gt"], 1)
                    for p4 in range(4):
                        v.wait_ge(S["s_dhc"], 64 * (p4 + 1))
                        for tt in (2 * p4, 2 * p4 + 1):
                            v.wait_ge(S[f"s_h1ld{tt % 2}"], 16 * (tt // 2 + 1))
                            v.scalar_tensor_tensor(
                                z2[:, tt % 2, :], ps[:, (tt % 2) * 2048:(tt % 2 + 1) * 2048], 1.0,
                                h1t_sb[:, tt % 2, :], ALU.mult, ALU.add,
                            ).then_inc(S["s_y2free"], 1)
                            v.drain()
                        for tt in (2 * p4, 2 * p4 + 1):
                            v.reduce_sum(rs[:, :], z2[:, tt % 2, :], axis=AX.X)
                            v.drain()
                            v.tensor_scalar_mul(negmean[:, :], rs[:, :], -1.0 / D)
                            v.drain()
                            v.tensor_scalar_add(z2[:, tt % 2, :], z2[:, tt % 2, :], negmean[:, :])
                            v.drain()
                            if tt >= 2:
                                v.wait_ge(S[f"s_out{tt % 2}"], 16 * (tt // 2))
                            v.scalar_tensor_tensor(
                                outst[:, tt % 2, :], z2[:, tt % 2, :], 1.0, z2[:, tt % 2, :],
                                ALU.mult, ALU.mult, accum_out=ssq[:, :],
                            )
                            v.drain()
                            v.tensor_scalar_mul(var_[:, :], ssq[:, :], 1.0 / D).then_inc(S["s_var2"], 1)
                            v.wait_ge(S["s_actr2"], tt + 1)
                            v.tensor_scalar_mul(outst[:, tt % 2, :], z2[:, tt % 2, :], rstd[:, :]).then_inc(S["s_ovr"], 1)
                            v.drain()

                @b3.sync
                def _(sy):
                    for ht in range(NHT):
                        sy.wait_ge(S["s_gt"], ht + 1)
                        sy.dma_start(
                            out=gtd.ap()[ht * 128:(ht + 1) * 128, :],
                            in_=gtw[:, ht % 3, :],
                        ).then_inc(S[f"s_gtw{ht % 3}"], 16)
                    sy.wait_ge(S["s_wope"], NTT)
                    sy.wait_ge(S["s_gtw0"], 16 * 22)
                    sy.wait_ge(S["s_gtw1"], 16 * 21)
                    sy.wait_ge(S["s_gtw2"], 16 * 21)
                    sy.wait_ge(S["s_wdw0"], 16 * NHT)
                    sy.wait_ge(S["s_h1d0"], 16 * (NTT // 2))
                    sy.wait_ge(S["s_h1d1"], 16 * (NTT // 2))
                    gtd_r = gtd.ap().rearrange("(hc k) t -> k hc t", k=128)
                    for p4 in range(4):
                        if p4 >= 2:
                            sy.wait_ge(S["s_dhc"], 64 * (p4 - 1))
                        for oc in range(8):
                            sy.dma_start(
                                out=gtp_view[:, p4 % 2, 8 * oc:8 * (oc + 1), :],
                                in_=gtd_r[:, 8 * oc:8 * (oc + 1), 256 * p4:256 * (p4 + 1)],
                            ).then_inc(S[f"s_gtr{p4 % 2}"], 16)
                    sy.wait_ge(S["s_out0"], 16 * (NTT // 2))
                    sy.wait_ge(S["s_out1"], 16 * (NTT // 2))

    return nc


_NC_CACHE = []


def kernel(L0_attn_x, L0_attn_Wq, L0_attn_Wo, L0_attn_bo,
           L0_ffn_w_gate, L0_ffn_w_up, L0_ffn_w_down):
    x = np.asarray(L0_attn_x, dtype=np.float32)
    Wq = np.asarray(L0_attn_Wq, dtype=np.float32)
    Wo = np.ascontiguousarray(np.asarray(L0_attn_Wo, dtype=np.float32))
    bov = np.ascontiguousarray(np.asarray(L0_attn_bo, dtype=np.float32))
    wgt = np.ascontiguousarray(
        np.asarray(L0_ffn_w_gate, dtype=np.float32).reshape(NHT, 128, NDC, 128).transpose(0, 3, 2, 1))
    wut = np.ascontiguousarray(
        np.asarray(L0_ffn_w_up, dtype=np.float32).reshape(NHT, 128, NDC, 128).transpose(0, 3, 2, 1))
    wdT = np.ascontiguousarray(np.asarray(L0_ffn_w_down, dtype=np.float32).T)
    eye = np.eye(128, dtype=np.float32)
    xT = [np.ascontiguousarray(x[b].T) for b in range(2)]

    in_maps = []
    for c in range(8):
        b, r4 = c // 4, c % 4
        wqt = np.ascontiguousarray(
            Wq[:, r4 * 512:(r4 + 1) * 512].reshape(NDC, 128, NHC, 128).transpose(1, 0, 2, 3))
        in_maps.append(dict(
            xt=xT[b],
            xres=np.ascontiguousarray(x[b, r4 * MINE:(r4 + 1) * MINE]),
            wq=wqt,
            wo=Wo, bo=bov, wg=wgt, wu=wut, wd=wdT, eye=eye,
        ))

    if not _NC_CACHE:
        _NC_CACHE.append(build())
    nc = _NC_CACHE[0]
    res = run_bass_kernel_spmd(nc, in_maps, core_ids=list(range(8)))
    pieces = [res.results[c]["out"] for c in range(8)]
    return np.concatenate(pieces, axis=0).reshape(2, N, D)


if __name__ == "__main__":
    rng = np.random.default_rng(1)
    x = rng.standard_normal((2, N, D), dtype=np.float32)
    Wq = (rng.standard_normal((D, D), dtype=np.float32) * D ** -0.5).astype(np.float32)
    Wo = (rng.standard_normal((D, D), dtype=np.float32) * D ** -0.5).astype(np.float32)
    bov = rng.standard_normal(D).astype(np.float32) * 0.1
    wgm = (rng.uniform(-1, 1, (HID, D)) / np.sqrt(D)).astype(np.float32)
    wum = (rng.uniform(-1, 1, (HID, D)) / np.sqrt(D)).astype(np.float32)
    wdm = (rng.uniform(-1, 1, (D, HID)) / np.sqrt(HID)).astype(np.float32)

    def ref():
        def ln(a):
            m = a.mean(-1, keepdims=True)
            v = a.var(-1, keepdims=True)
            return (a - m) / np.sqrt(v + EPS)
        q = x @ Wq
        qh = q.reshape(2, N, 16, HD).transpose(0, 2, 1, 3)
        s = np.einsum("bhnd,bhmd->bhnm", qh, qh) * HD ** -0.5
        s = s - s.max(-1, keepdims=True)
        a = np.exp(s)
        a /= a.sum(-1, keepdims=True)
        y = np.einsum("bhnm,bhmd->bhnd", a, qh).transpose(0, 2, 1, 3).reshape(2, N, D)
        h = ln(x + y @ Wo + bov)
        gate = h @ wgm.T
        up = h @ wum.T
        gated = gate / (1 + np.exp(-gate)) * up
        return ln(h + gated @ wdm.T)

    got = kernel(L0_attn_x=x, L0_attn_Wq=Wq, L0_attn_Wo=Wo, L0_attn_bo=bov,
                 L0_ffn_w_gate=wgm, L0_ffn_w_up=wum, L0_ffn_w_down=wdm)
    want = ref()
    err = np.sqrt(np.mean((got - want) ** 2) / np.mean(want ** 2))
    print("rel l2:", err)
    print("max abs:", np.abs(got - want).max())


# version 07
# speedup vs baseline: 1.1221x; 1.0288x over previous
import numpy as np
import concourse.bass as bass
import concourse.mybir as mybir
from concourse.bass_utils import run_bass_kernel_spmd

F32 = mybir.dt.float32
BF16 = mybir.dt.bfloat16
AF = mybir.ActivationFunctionType
ALU = mybir.AluOpType
AX = mybir.AxisListType

N = 4096
D = 2048
HD = 128
NHC = 4
MINE = 1024
HID = 8192
NDC = D // 128
NHT = HID // 128
NKT = N // 128
NTT = MINE // 128
ISCALE = float(1.0 / np.sqrt(HD))
EPS = 1e-5
GROUPS = [[0, 1, 2, 3], [4, 5, 6, 7]]


def build():
    nc = bass.Bass("TRN2", target_bir_lowering=False)

    xt = nc.dram_tensor("xt", [D, N], F32, kind="ExternalInput").ap()
    xres = nc.dram_tensor("xres", [MINE, D], F32, kind="ExternalInput").ap()
    wq = nc.dram_tensor("wq", [128, NDC, NHC, HD], F32, kind="ExternalInput").ap()
    wo = nc.dram_tensor("wo", [D, D], F32, kind="ExternalInput").ap()
    bo = nc.dram_tensor("bo", [D], F32, kind="ExternalInput").ap()
    wg = nc.dram_tensor("wg", [NHT, 128, NDC, 128], F32, kind="ExternalInput").ap()
    wu = nc.dram_tensor("wu", [NHT, 128, NDC, 128], F32, kind="ExternalInput").ap()
    wd = nc.dram_tensor("wd", [HID, D], F32, kind="ExternalInput").ap()
    eye = nc.dram_tensor("eye", [128, 128], F32, kind="ExternalInput").ap()
    out = nc.dram_tensor("out", [MINE, D], F32, kind="ExternalOutput").ap()

    ag_send = [nc.dram_tensor(f"ag_send{h}", [HD, N], BF16) for h in range(NHC)]
    ag_recv = [nc.dram_tensor(f"ag_recv{h}", [4 * HD, N], BF16) for h in range(NHC)]
    zdr = nc.dram_tensor("zdr", [NHC, N], F32)
    h1d = nc.dram_tensor("h1d", [MINE, D], F32)
    gtd = nc.dram_tensor("gtd", [HID, MINE], BF16)
    wdb = nc.dram_tensor("wdb", [HID, D], BF16)

    sem_names = [
        "s_pre", "s_xt0", "s_xt1", "s_xt2", "s_xt3",
        "s_qev", "s_peq", "s_vt", "s_vte",
        "s_sS", "s_exp", "s_zac", "s_av", "s_otev", "s_zrow", "s_zw", "s_zbc", "s_otA", "s_otn",
        "s_ag", "s_cc",
        "s_wo", "s_otg0", "s_otg1", "s_otg2", "s_otg3", "s_xr0", "s_xr1", "s_bo", "s_wope", "s_ypsfree",
        "s_var1", "s_actr", "s_h1f", "s_h1d0", "s_h1d1", "s_tp", "s_tpe",
        "s_wg0", "s_wg1", "s_wg2", "s_gpe", "s_upe", "s_silu", "s_gt",
        "s_gtw0", "s_gtw1", "s_gtw2",
        "s_wdr0", "s_wdr1", "s_wdr2", "s_wdr3",
        "s_wdw0", "s_wdw1", "s_wdw2", "s_wdw3",
        "s_gtr0", "s_gtr1", "s_wdd0", "s_wdd1", "s_wdd2", "s_wdd3", "s_dhc", "s_y2free",
        "s_var2", "s_actr2", "s_h1ld0", "s_h1ld1", "s_ovr", "s_out0", "s_out1",
    ]

    import contextlib
    with contextlib.ExitStack() as est:
        S = {n: est.enter_context(nc.semaphore(n)) for n in sem_names}
        ps = est.enter_context(nc.psum_tensor("ps", [128, 4096], F32))

        def bank(i, n=1):
            return ps[:, 512 * i: 512 * (i + n)]

        def bank_bf(i):
            return ps[:, 512 * i: 512 * i + 64].bitcast(BF16)

        def bank32(i):
            return ps[:, 512 * i: 512 * i + 128]

        eye_bf = est.enter_context(nc.sbuf_tensor("eye_bf", [128, 128], BF16))
        eye32 = est.enter_context(nc.sbuf_tensor("eye32", [128, 128], F32))
        big64 = est.enter_context(nc.sbuf_tensor("big64", [128, 2 * NHT * 256], BF16))
        wo_view = big64[:, :].rearrange("p (dc n) -> p dc n", dc=NDC)
        gtp_view = big64[:, :].rearrange("p (b hc m) -> p b hc m", b=2, hc=NHT)
        rs = est.enter_context(nc.sbuf_tensor("rs", [128, 1], F32))
        negmean = est.enter_context(nc.sbuf_tensor("negmean", [128, 1], F32))
        ssq = est.enter_context(nc.sbuf_tensor("ssq", [128, 1], F32))
        var_ = est.enter_context(nc.sbuf_tensor("var_", [128, 1], F32))
        lnv = est.enter_context(nc.sbuf_tensor("lnv", [128, 1], F32))
        rstd = est.enter_context(nc.sbuf_tensor("rstd", [128, 1], F32))
        epsv = est.enter_context(nc.sbuf_tensor("epsv", [128, 1], F32))

        regs = {}

        with (
            nc.sbuf_tensor("qT", [128, NHC, N], BF16) as qT,
            nc.sbuf_tensor("v_sb", [128, NHC, NKT, HD], BF16) as v_sb,
            nc.sbuf_tensor("xt_sb", [128, 3, 1024], BF16) as xt_sb,
            nc.sbuf_tensor("wq_sb", [128, NDC, NHC, HD], BF16) as wq_sb,
            nc.sbuf_tensor("P_sb", [128, 2, 1024], BF16) as P_sb,
            nc.sbuf_tensor("ot_raw", [128, 2, N], F32) as ot_raw,
            nc.sbuf_tensor("zp", [128, NHC, 4, NKT], F32) as zp,
            nc.sbuf_tensor("zs", [128, NKT], F32) as zs,
            nc.sbuf_tensor("zinv_s", [128, NKT], F32) as zinv_s,
            nc.sbuf_tensor("zrowt", [32, 128], F32) as zrowt,
            nc.sbuf_tensor("zinvb", [128, N // 2], F32) as zinvb,
            nc.sbuf_tensor("otn", [128, N // 2], BF16) as otn,
            nc.Block() as blk,
        ):
            @blk.gpsimd
            def _(g):
                g.dma_start(out=wq_sb[:, :, :, :].rearrange("p dc h c -> p (dc h c)"),
                            in_=wq.rearrange("p dc h c -> p (dc h c)")).then_inc(S["s_pre"], 16)
                g.dma_start(out=eye_bf[:, :], in_=eye).then_inc(S["s_pre"], 16)
                for s4 in range(4):
                    for dc in range(NDC):
                        j = 16 * s4 + dc
                        if j >= 3:
                            g.wait_ge(S["s_peq"], j - 2)
                        g.dma_start(
                            out=xt_sb[:, j % 3, :],
                            in_=xt[dc * 128:(dc + 1) * 128, s4 * 1024:(s4 + 1) * 1024],
                        ).then_inc(S[f"s_xt{j % 3}"], 16)
                for dq in range(4):
                    g.dma_start(
                        out=wo_view[:, dq * 4:(dq + 1) * 4, :],
                        in_=wo[dq * 512:(dq + 1) * 512, :].rearrange("(dc p) n -> p dc n", p=128),
                    ).then_inc(S["s_wo"], 16)
                for h in range(NHC):
                    g.wait_ge(S["s_otA"], h + 1)
                    g.dma_start(out=ag_send[h].ap()[:, 0:N // 2], in_=otn[:, :]).then_inc(S["s_ag"], 16)
                    g.wait_ge(S["s_otn"], h + 1)
                    g.dma_start(out=ag_send[h].ap()[:, N // 2:N], in_=otn[:, :]).then_inc(S["s_ag"], 16)
                    g.wait_ge(S["s_ag"], 32 * (h + 1))
                    g.collective_compute(
                        "AllGather", ALU.bypass, replica_groups=GROUPS,
                        ins=[ag_send[h].ap()], outs=[ag_recv[h].ap()],
                    ).then_inc(S["s_cc"], 1)


            @blk.tensor
            def _(t):
                t.wait_ge(S["s_pre"], 32)
                for s4 in range(4):
                    for dc in range(NDC):
                        j = 16 * s4 + dc
                        t.wait_ge(S[f"s_xt{j % 3}"], 16 * (j // 3 + 1))
                        if dc == 0 and s4 > 0:
                            t.wait_ge(S["s_qev"], 4 * s4)
                        ins = None
                        for h in range(NHC):
                            for half in range(2):
                                ins = t.matmul(
                                    bank(2 * h + half),
                                    wq_sb[:, dc, h, :],
                                    xt_sb[:, j % 3, 512 * half:512 * (half + 1)],
                                    start=(dc == 0), stop=(dc == NDC - 1),
                                )
                        ins.then_inc(S["s_peq"], 1)
                for h in range(NHC):
                    for kt in range(NKT):
                        j = NKT * h + kt
                        if j == 0:
                            t.wait_ge(S["s_qev"], 16)
                        if j >= 2:
                            t.wait_ge(S["s_vte"], j - 1)
                        t.transpose(
                            bank_bf(j % 2),
                            qT[:, h, kt * 128:(kt + 1) * 128],
                            eye_bf[:, :],
                        ).then_inc(S["s_vt"], 1)
                for gidx in range(16):
                    h, qq = gidx // 4, gidx % 4
                    po = gidx % 2

                    def emit_S(kt, gidx=gidx, h=h, qq=qq):
                        idx = 32 * gidx + kt
                        if idx == 0:
                            t.wait_ge(S["s_vte"], 128)
                        par = kt % 2
                        ins = None
                        for half in range(2):
                            ins = t.matmul(
                                bank(2 * par + half),
                                qT[:, h, kt * 128:(kt + 1) * 128],
                                qT[:, h, qq * 1024 + 512 * half: qq * 1024 + 512 * (half + 1)],
                                start=True, stop=True,
                            )
                        ins.then_inc(S["s_sS"], 1)

                    def emit_AV(kt, gidx=gidx, h=h, po=po):
                        idx = 32 * gidx + kt
                        t.wait_ge(S["s_exp"], idx + 1)
                        if kt == 0 and gidx >= 2:
                            t.wait_ge(S["s_otev"], gidx - 1)
                        ins = None
                        for half in range(2):
                            ins = t.matmul(
                                bank(4 + 2 * po + half),
                                v_sb[:, h, kt, :],
                                P_sb[:, kt % 2, 512 * half:512 * (half + 1)],
                                start=(kt == 0), stop=(kt == NKT - 1),
                            )
                        ins.then_inc(S["s_av"], 1)

                    emit_S(0)
                    emit_S(1)
                    for kt in range(NKT):
                        emit_AV(kt)
                        if kt + 2 < NKT:
                            emit_S(kt + 2)

            @blk.scalar
            def _(sc):
                for gidx in range(16):
                    h, qq = gidx // 4, gidx % 4
                    for kt in range(NKT):
                        idx = 32 * gidx + kt
                        par = kt % 2
                        sc.wait_ge(S["s_sS"], idx + 1)
                        if idx >= 2:
                            sc.wait_ge(S["s_av"], idx - 1)
                        sc.activation(
                            P_sb[:, par, :],
                            ps[:, par * 1024:(par + 1) * 1024],
                            AF.Exp, scale=ISCALE,
                            accum_out=zp[:, h, qq, kt:kt + 1].opt(),
                        ).then_inc(S["s_exp"], 1)

            @blk.vector
            def _(v):
                v.memset(epsv[:, :], EPS)
                for s4 in range(4):
                    v.wait_ge(S["s_peq"], 16 * (s4 + 1))
                    for h in range(NHC):
                        v.tensor_copy(
                            qT[:, h, s4 * 1024:(s4 + 1) * 1024], bank(2 * h, 2)
                        ).then_inc(S["s_qev"], 1)
                for h in range(NHC):
                    for kt in range(NKT):
                        j = NKT * h + kt
                        v.wait_ge(S["s_vt"], j + 1)
                        v.tensor_copy(v_sb[:, h, kt, :], bank_bf(j % 2)).then_inc(S["s_vte"], 1)

                def emit_otn(hp):
                    v.wait_ge(S["s_zbc"], 32 * hp + 16)
                    if hp >= 1:
                        v.wait_ge(S["s_ag"], 32 * hp)
                    ins = None
                    for q2 in range(4):
                        if q2 == 2:
                            ins.then_inc(S["s_otA"], 1)
                            v.wait_ge(S["s_zbc"], 32 * hp + 32)
                            v.wait_ge(S["s_ag"], 32 * hp + 16)
                        elif q2 > 0:
                            v.drain()
                        ins = v.tensor_mul(
                            otn[:, (q2 % 2) * 1024:(q2 % 2 + 1) * 1024],
                            ot_raw[:, hp % 2, q2 * 1024:(q2 + 1) * 1024],
                            zinvb[:, (q2 % 2) * 1024:(q2 % 2 + 1) * 1024],
                        )
                    ins.then_inc(S["s_otn"], 1)

                for gidx in range(16):
                    h, qq = gidx // 4, gidx % 4
                    po = gidx % 2
                    v.wait_ge(S["s_av"], 32 * (gidx + 1))
                    v.tensor_copy(
                        ot_raw[:, h % 2, qq * 1024:(qq + 1) * 1024], bank(4 + 2 * po, 2)
                    ).then_inc(S["s_otev"], 1)
                    if qq == 3:
                        v.tensor_add(zs[:, :], zp[:, h, 0, :], zp[:, h, 1, :])
                        v.drain()
                        v.tensor_add(zs[:, :], zs[:, :], zp[:, h, 2, :])
                        v.drain()
                        v.tensor_add(zs[:, :], zs[:, :], zp[:, h, 3, :])
                        v.drain()
                        v.reciprocal(zinv_s[:, :], zs[:, :])
                        v.drain()
                        ins = None
                        for pb in range(4):
                            if pb > 0:
                                v.drain()
                            ins = v.transpose(
                                zrowt[0:32, 32 * pb:32 * (pb + 1)],
                                zinv_s[32 * pb:32 * (pb + 1), 0:32],
                            )
                        ins.then_inc(S["s_zrow"], 1)
                    if qq == 0 and gidx >= 4:
                        emit_otn(gidx // 4 - 1)
                emit_otn(3)

            @blk.sync
            def _(sy):
                pid = sy.partition_id()
                regs["r4"] = sy.compute_val(pid % 4)
                sy.dma_start(out=eye32[:, :], in_=eye).then_inc(S["s_bo"], 16)
                for h in range(NHC):
                    sy.wait_ge(S["s_zrow"], h + 1)
                    sy.dma_start(
                        out=zdr.ap()[h].rearrange("(kt p) -> kt p", p=128),
                        in_=zrowt[0:32, :],
                    ).then_inc(S["s_zw"], 16)
                    sy.wait_ge(S["s_zw"], 16 * (h + 1))
                    if h >= 1:
                        sy.wait_ge(S["s_otn"], h)
                    sy.dma_start(
                        out=zinvb[:, :],
                        in_=zdr.ap()[h][0:N // 2].partition_broadcast(128),
                    ).then_inc(S["s_zbc"], 16)
                    sy.wait_ge(S["s_otA"], h + 1)
                    sy.dma_start(
                        out=zinvb[:, :],
                        in_=zdr.ap()[h][N // 2:N].partition_broadcast(128),
                    ).then_inc(S["s_zbc"], 16)

        with (
            nc.sbuf_tensor("wg_sb", [128, 3, NDC, 128], BF16) as wg_sb,
            nc.sbuf_tensor("wu_sb", [128, 3, NDC, 128], BF16) as wu_sb,
            nc.sbuf_tensor("h1T", [128, NDC, MINE], BF16) as h1T,
        ):
            with (
                nc.sbuf_tensor("otg", [128, NDC, MINE], BF16) as otg,
                nc.sbuf_tensor("xres_sb", [128, 2, D], F32) as xres_sb,
                nc.sbuf_tensor("bo_bc", [128, D], F32) as bo_bc,
                nc.sbuf_tensor("z1", [128, D], F32) as z1,
                nc.sbuf_tensor("h1f", [128, 2, D], F32) as h1f,
                nc.Block() as b2,
            ):
                @b2.sync
                def _(sy):
                    def otg_load(hl):
                        for r in range(4):
                            sy.dma_start(
                                out=otg[:, 4 * r + hl, :],
                                in_=ag_recv[hl].ap()[r * 128:(r + 1) * 128, bass.ts(regs["r4"], MINE)],
                            ).then_inc(S[f"s_otg{hl}"], 16)
                    for hl in range(3):
                        sy.wait_ge(S["s_cc"], hl + 1)
                        otg_load(hl)
                    sy.dma_start(out=bo_bc[:, :], in_=bo.partition_broadcast(128)).then_inc(S["s_bo"], 16)
                    sy.wait_ge(S["s_cc"], NHC)
                    otg_load(3)
                    for tt in range(NTT):
                        if tt >= 2:
                            sy.wait_ge(S["s_h1f"], tt - 1)
                        sy.dma_start(
                            out=xres_sb[:, tt % 2, :],
                            in_=xres[tt * 128:(tt + 1) * 128, :],
                        ).then_inc(S[f"s_xr{tt % 2}"], 16)

                @b2.tensor
                def _(t):
                    t.wait_ge(S["s_exp"], 512)
                    t.wait_ge(S["s_otev"], 16)

                    def emit_tr(tt_src):
                        t.wait_ge(S["s_h1f"], tt_src + 1)
                        for j16 in range(16):
                            jj = 16 * tt_src + j16
                            if jj == 0:
                                pass
                            if jj >= 2:
                                t.wait_ge(S["s_tpe"], jj - 1)
                            t.transpose(
                                bank32(4 + jj % 2),
                                h1f[:, tt_src % 2, j16 * 128:(j16 + 1) * 128],
                                eye32[:, :],
                            ).then_inc(S["s_tp"], 1)

                    dc_order = [d for d in range(NDC) if d % 4 != 3] + [d for d in range(NDC) if d % 4 == 3]
                    for tt in range(NTT):
                        if tt == 0:
                            t.wait_ge(S["s_wo"], 16 * 4)
                        if tt >= 1:
                            t.wait_ge(S["s_ypsfree"], tt)
                        ins = None
                        for di, dc in enumerate(dc_order):
                            if tt == 0:
                                t.wait_ge(S[f"s_otg{dc % 4}"], 16 * 4)
                            for half in range(4):
                                ins = t.matmul(
                                    bank(half),
                                    otg[:, dc, tt * 128:(tt + 1) * 128],
                                    wo_view[:, dc, 512 * half:512 * (half + 1)],
                                    start=(di == 0), stop=(di == NDC - 1),
                                )
                        ins.then_inc(S["s_wope"], 1)
                        if tt >= 1:
                            emit_tr(tt - 1)
                    emit_tr(NTT - 1)

                @b2.vector
                def _(v):
                    def emit_trev(tt_src):
                        for j16 in range(16):
                            jj = 16 * tt_src + j16
                            v.wait_ge(S["s_tp"], jj + 1)
                            v.tensor_copy(
                                h1T[:, j16, tt_src * 128:(tt_src + 1) * 128],
                                bank32(4 + jj % 2),
                            ).then_inc(S["s_tpe"], 1)

                    for tt in range(NTT):
                        v.wait_ge(S["s_wope"], tt + 1)
                        if tt == 0:
                            v.wait_ge(S["s_bo"], 32)
                        v.scalar_tensor_tensor(
                            z1[:, :], ps[:, 0:2048], 1.0, bo_bc[:, :], ALU.mult, ALU.add
                        ).then_inc(S["s_ypsfree"], 1)
                        v.drain()
                        v.wait_ge(S[f"s_xr{tt % 2}"], 16 * (tt // 2 + 1))
                        v.tensor_add(z1[:, :], z1[:, :], xres_sb[:, tt % 2, :])
                        v.drain()
                        v.reduce_sum(rs[:, :], z1[:, :], axis=AX.X)
                        v.drain()
                        v.tensor_scalar_mul(negmean[:, :], rs[:, :], -1.0 / D)
                        v.drain()
                        v.tensor_scalar_add(z1[:, :], z1[:, :], negmean[:, :])
                        v.drain()
                        if tt >= 2:
                            v.wait_ge(S[f"s_h1d{tt % 2}"], 16 * (tt // 2))
                            v.wait_ge(S["s_tp"], 16 * (tt - 1))
                        v.scalar_tensor_tensor(
                            h1f[:, tt % 2, :], z1[:, :], 1.0, z1[:, :], ALU.mult, ALU.mult,
                            accum_out=ssq[:, :],
                        )
                        v.drain()
                        v.tensor_scalar_mul(var_[:, :], ssq[:, :], 1.0 / D).then_inc(S["s_var1"], 1)
                        v.wait_ge(S["s_actr"], tt + 1)
                        v.tensor_scalar_mul(h1f[:, tt % 2, :], z1[:, :], rstd[:, :]).then_inc(S["s_h1f"], 1)
                        v.drain()
                        if tt >= 1:
                            emit_trev(tt - 1)
                    emit_trev(NTT - 1)

                @b2.scalar
                def _(sc):
                    for tt in range(NTT):
                        sc.wait_ge(S["s_var1"], tt + 1)
                        sc.activation(lnv[:, :], var_[:, :], AF.Ln, bias=epsv[:, :])
                        sc.drain()
                        sc.activation(rstd[:, :], lnv[:, :], AF.Exp, scale=-0.5).then_inc(S["s_actr"], 1)
                        sc.drain()
                        sc.wait_ge(S["s_h1f"], tt + 1)
                        sc.dma_start(
                            out=h1d.ap()[tt * 128:(tt + 1) * 128, :],
                            in_=h1f[:, tt % 2, :],
                        ).then_inc(S[f"s_h1d{tt % 2}"], 16)

            with (
                nc.sbuf_tensor("sg", [128, 2, 1024], F32) as sg,
                nc.sbuf_tensor("wdstage", [128, 2, D], BF16) as wdstage,
                nc.sbuf_tensor("gtw", [128, 3, 1024], BF16) as gtw,
                nc.sbuf_tensor("wdd", [128, 4, D], BF16) as wdd,
                nc.sbuf_tensor("h1t_sb", [128, 2, D], F32) as h1t_sb,
                nc.sbuf_tensor("outst", [128, 2, D], F32) as outst,
                nc.sbuf_tensor("z2", [128, 2, D], F32) as z2,
                nc.Block() as b3,
            ):
                @b3.gpsimd
                def _(g):
                    def load_wgu(ht):
                        for w_ap, w_sb in ((wg, wg_sb), (wu, wu_sb)):
                            g.dma_start(
                                out=w_sb[:, ht % 3, :, :].rearrange("p dc c -> p (dc c)"),
                                in_=w_ap[ht].rearrange("p dc c -> p (dc c)"),
                            ).then_inc(S[f"s_wg{ht % 3}"], 16)

                    load_wgu(0)
                    load_wgu(1)
                    load_wgu(2)
                    g.wait_ge(S["s_tpe"], 128)
                    g.wait_ge(S["s_h1d0"], 16 * (NTT // 2))
                    g.wait_ge(S["s_h1d1"], 16 * (NTT // 2))
                    for ht in range(NHT):
                        if ht + 3 < NHT:
                            g.wait_ge(S["s_upe"], ht + 1)
                            load_wgu(ht + 3)
                        if ht >= 2:
                            g.wait_ge(S[f"s_wdw{ht % 2}"], 16 * (ht // 2))
                        g.dma_start(
                            out=wdstage[:, ht % 2, :],
                            in_=wd[ht * 128:(ht + 1) * 128, :],
                        ).then_inc(S[f"s_wdr{ht % 2}"], 16)
                        hcc = ht - 1
                        if hcc >= 0:
                            g.wait_ge(S[f"s_wdr{hcc % 2}"], 16 * (hcc // 2 + 1))
                            g.dma_start(
                                out=wdb.ap()[hcc * 128:(hcc + 1) * 128, :],
                                in_=wdstage[:, hcc % 2, :],
                            ).then_inc(S[f"s_wdw{hcc % 2}"], 16)
                    hcc = NHT - 1
                    g.wait_ge(S[f"s_wdr{hcc % 2}"], 16 * (hcc // 2 + 1))
                    g.dma_start(
                        out=wdb.ap()[hcc * 128:(hcc + 1) * 128, :],
                        in_=wdstage[:, hcc % 2, :],
                    ).then_inc(S[f"s_wdw{hcc % 2}"], 16)

                @b3.tensor
                def _(t):
                    for ht in range(NHT):
                        t.wait_ge(S[f"s_wg{ht % 3}"], 32 * (ht // 3 + 1))
                        if ht == 0:
                            t.wait_ge(S["s_tpe"], 128)
                        if ht >= 2:
                            t.wait_ge(S["s_silu"], ht - 1)
                        ins = None
                        for dc in range(NDC):
                            for half in range(2):
                                ins = t.matmul(
                                    bank(2 * (ht % 2) + half),
                                    wg_sb[:, ht % 3, dc, :],
                                    h1T[:, dc, 512 * half:512 * (half + 1)],
                                    start=(dc == 0), stop=(dc == NDC - 1),
                                )
                        ins.then_inc(S["s_gpe"], 1)
                        if ht >= 2:
                            t.wait_ge(S["s_gt"], ht - 1)
                        ins = None
                        for dc in range(NDC):
                            for half in range(2):
                                ins = t.matmul(
                                    bank(4 + 2 * (ht % 2) + half),
                                    wu_sb[:, ht % 3, dc, :],
                                    h1T[:, dc, 512 * half:512 * (half + 1)],
                                    start=(dc == 0), stop=(dc == NDC - 1),
                                )
                        ins.then_inc(S["s_upe"], 1)
                    t.wait_ge(S["s_silu"], NHT)
                    t.wait_ge(S["s_gt"], NHT)
                    for p4 in range(4):
                        t.wait_ge(S[f"s_gtr{p4 % 2}"], 128 * (p4 // 2 + 1))
                        if p4 >= 1:
                            t.wait_ge(S["s_y2free"], 2 * p4)
                        for hc in range(NHT):
                            gidx2 = 64 * p4 + hc
                            t.wait_ge(S[f"s_wdd{gidx2 % 4}"], 16 * (gidx2 // 4 + 1))
                            ins = None
                            for side in range(2):
                                for half in range(4):
                                    ins = t.matmul(
                                        bank(4 * side + half),
                                        gtp_view[:, p4 % 2, hc, 128 * side:128 * (side + 1)],
                                        wdd[:, gidx2 % 4, 512 * half:512 * (half + 1)],
                                        start=(hc == 0), stop=(hc == NHT - 1),
                                    )
                            ins.then_inc(S["s_dhc"], 1)

                @b3.scalar
                def _(sc):
                    for ht in range(NHT):
                        sc.wait_ge(S["s_gpe"], ht + 1)
                        if ht >= 2:
                            sc.wait_ge(S["s_gt"], ht - 1)
                        sc.activation(
                            sg[:, ht % 2, :],
                            ps[:, (ht % 2) * 1024:(ht % 2 + 1) * 1024],
                            AF.Silu,
                        ).then_inc(S["s_silu"], 1)
                    sc.wait_ge(S["s_gtw0"], 16 * 22)
                    sc.wait_ge(S["s_gtw1"], 16 * 21)
                    sc.wait_ge(S["s_gtw2"], 16 * 21)
                    sc.wait_ge(S["s_wdw0"], 16 * (NHT // 2))
                    sc.wait_ge(S["s_wdw1"], 16 * (NHT // 2))
                    sc.wait_ge(S["s_h1d0"], 16 * (NTT // 2))
                    sc.wait_ge(S["s_h1d1"], 16 * (NTT // 2))
                    for p4 in range(4):
                        for tt in (2 * p4, 2 * p4 + 1):
                            if tt >= 2:
                                sc.wait_ge(S["s_y2free"], tt - 1)
                            sc.dma_start(
                                out=h1t_sb[:, tt % 2, :],
                                in_=h1d.ap()[tt * 128:(tt + 1) * 128, :],
                            ).then_inc(S[f"s_h1ld{tt % 2}"], 16)
                        for hc in range(NHT):
                            gidx2 = 64 * p4 + hc
                            if gidx2 >= 4:
                                sc.wait_ge(S["s_dhc"], gidx2 - 3)
                            sc.dma_start(
                                out=wdd[:, gidx2 % 4, :],
                                in_=wdb.ap()[hc * 128:(hc + 1) * 128, :],
                            ).then_inc(S[f"s_wdd{gidx2 % 4}"], 16)
                        for tt in (2 * p4, 2 * p4 + 1):
                            sc.wait_ge(S["s_var2"], tt + 1)
                            sc.activation(lnv[:, :], var_[:, :], AF.Ln, bias=epsv[:, :])
                            sc.drain()
                            sc.activation(rstd[:, :], lnv[:, :], AF.Exp, scale=-0.5).then_inc(S["s_actr2"], 1)
                            sc.drain()
                            sc.wait_ge(S["s_ovr"], tt + 1)
                            sc.dma_start(
                                out=out[tt * 128:(tt + 1) * 128, :],
                                in_=outst[:, tt % 2, :],
                            ).then_inc(S[f"s_out{tt % 2}"], 16)

                @b3.vector
                def _(v):
                    for ht in range(NHT):
                        v.wait_ge(S["s_silu"], ht + 1)
                        v.wait_ge(S["s_upe"], ht + 1)
                        if ht >= 3:
                            v.wait_ge(S[f"s_gtw{ht % 3}"], 16 * (ht // 3))
                        v.tensor_mul(
                            gtw[:, ht % 3, :],
                            sg[:, ht % 2, :],
                            ps[:, (4 + 2 * (ht % 2)) * 512:(6 + 2 * (ht % 2)) * 512],
                        ).then_inc(S["s_gt"], 1)
                    for p4 in range(4):
                        v.wait_ge(S["s_dhc"], 64 * (p4 + 1))
                        for tt in (2 * p4, 2 * p4 + 1):
                            v.wait_ge(S[f"s_h1ld{tt % 2}"], 16 * (tt // 2 + 1))
                            v.scalar_tensor_tensor(
                                z2[:, tt % 2, :], ps[:, (tt % 2) * 2048:(tt % 2 + 1) * 2048], 1.0,
                                h1t_sb[:, tt % 2, :], ALU.mult, ALU.add,
                            ).then_inc(S["s_y2free"], 1)
                            v.drain()
                        for tt in (2 * p4, 2 * p4 + 1):
                            v.reduce_sum(rs[:, :], z2[:, tt % 2, :], axis=AX.X)
                            v.drain()
                            v.tensor_scalar_mul(negmean[:, :], rs[:, :], -1.0 / D)
                            v.drain()
                            v.tensor_scalar_add(z2[:, tt % 2, :], z2[:, tt % 2, :], negmean[:, :])
                            v.drain()
                            if tt >= 2:
                                v.wait_ge(S[f"s_out{tt % 2}"], 16 * (tt // 2))
                            v.scalar_tensor_tensor(
                                outst[:, tt % 2, :], z2[:, tt % 2, :], 1.0, z2[:, tt % 2, :],
                                ALU.mult, ALU.mult, accum_out=ssq[:, :],
                            )
                            v.drain()
                            v.tensor_scalar_mul(var_[:, :], ssq[:, :], 1.0 / D).then_inc(S["s_var2"], 1)
                            v.wait_ge(S["s_actr2"], tt + 1)
                            v.tensor_scalar_mul(outst[:, tt % 2, :], z2[:, tt % 2, :], rstd[:, :]).then_inc(S["s_ovr"], 1)
                            v.drain()

                @b3.sync
                def _(sy):
                    for ht in range(NHT):
                        sy.wait_ge(S["s_gt"], ht + 1)
                        sy.dma_start(
                            out=gtd.ap()[ht * 128:(ht + 1) * 128, :],
                            in_=gtw[:, ht % 3, :],
                        ).then_inc(S[f"s_gtw{ht % 3}"], 16)
                    sy.wait_ge(S["s_wope"], NTT)
                    sy.wait_ge(S["s_gtw0"], 16 * 22)
                    sy.wait_ge(S["s_gtw1"], 16 * 21)
                    sy.wait_ge(S["s_gtw2"], 16 * 21)
                    sy.wait_ge(S["s_wdw0"], 16 * (NHT // 2))
                    sy.wait_ge(S["s_wdw1"], 16 * (NHT // 2))
                    sy.wait_ge(S["s_h1d0"], 16 * (NTT // 2))
                    sy.wait_ge(S["s_h1d1"], 16 * (NTT // 2))
                    gtd_r = gtd.ap().rearrange("(hc k) t -> k hc t", k=128)
                    for p4 in range(4):
                        if p4 >= 2:
                            sy.wait_ge(S["s_dhc"], 64 * (p4 - 1))
                        for oc in range(8):
                            sy.dma_start(
                                out=gtp_view[:, p4 % 2, 8 * oc:8 * (oc + 1), :],
                                in_=gtd_r[:, 8 * oc:8 * (oc + 1), 256 * p4:256 * (p4 + 1)],
                            ).then_inc(S[f"s_gtr{p4 % 2}"], 16)
                    sy.wait_ge(S["s_out0"], 16 * (NTT // 2))
                    sy.wait_ge(S["s_out1"], 16 * (NTT // 2))

    return nc


_NC_CACHE = []


def kernel(L0_attn_x, L0_attn_Wq, L0_attn_Wo, L0_attn_bo,
           L0_ffn_w_gate, L0_ffn_w_up, L0_ffn_w_down):
    x = np.asarray(L0_attn_x, dtype=np.float32)
    Wq = np.asarray(L0_attn_Wq, dtype=np.float32)
    Wo = np.ascontiguousarray(np.asarray(L0_attn_Wo, dtype=np.float32))
    bov = np.ascontiguousarray(np.asarray(L0_attn_bo, dtype=np.float32))
    wgt = np.ascontiguousarray(
        np.asarray(L0_ffn_w_gate, dtype=np.float32).reshape(NHT, 128, NDC, 128).transpose(0, 3, 2, 1))
    wut = np.ascontiguousarray(
        np.asarray(L0_ffn_w_up, dtype=np.float32).reshape(NHT, 128, NDC, 128).transpose(0, 3, 2, 1))
    wdT = np.ascontiguousarray(np.asarray(L0_ffn_w_down, dtype=np.float32).T)
    eye = np.eye(128, dtype=np.float32)
    xT = [np.ascontiguousarray(x[b].T) for b in range(2)]

    in_maps = []
    for c in range(8):
        b, r4 = c // 4, c % 4
        wqt = np.ascontiguousarray(
            Wq[:, r4 * 512:(r4 + 1) * 512].reshape(NDC, 128, NHC, 128).transpose(1, 0, 2, 3))
        in_maps.append(dict(
            xt=xT[b],
            xres=np.ascontiguousarray(x[b, r4 * MINE:(r4 + 1) * MINE]),
            wq=wqt,
            wo=Wo, bo=bov, wg=wgt, wu=wut, wd=wdT, eye=eye,
        ))

    if not _NC_CACHE:
        _NC_CACHE.append(build())
    nc = _NC_CACHE[0]
    res = run_bass_kernel_spmd(nc, in_maps, core_ids=list(range(8)))
    pieces = [res.results[c]["out"] for c in range(8)]
    return np.concatenate(pieces, axis=0).reshape(2, N, D)


if __name__ == "__main__":
    rng = np.random.default_rng(1)
    x = rng.standard_normal((2, N, D), dtype=np.float32)
    Wq = (rng.standard_normal((D, D), dtype=np.float32) * D ** -0.5).astype(np.float32)
    Wo = (rng.standard_normal((D, D), dtype=np.float32) * D ** -0.5).astype(np.float32)
    bov = rng.standard_normal(D).astype(np.float32) * 0.1
    wgm = (rng.uniform(-1, 1, (HID, D)) / np.sqrt(D)).astype(np.float32)
    wum = (rng.uniform(-1, 1, (HID, D)) / np.sqrt(D)).astype(np.float32)
    wdm = (rng.uniform(-1, 1, (D, HID)) / np.sqrt(HID)).astype(np.float32)

    def ref():
        def ln(a):
            m = a.mean(-1, keepdims=True)
            v = a.var(-1, keepdims=True)
            return (a - m) / np.sqrt(v + EPS)
        q = x @ Wq
        qh = q.reshape(2, N, 16, HD).transpose(0, 2, 1, 3)
        s = np.einsum("bhnd,bhmd->bhnm", qh, qh) * HD ** -0.5
        s = s - s.max(-1, keepdims=True)
        a = np.exp(s)
        a /= a.sum(-1, keepdims=True)
        y = np.einsum("bhnm,bhmd->bhnd", a, qh).transpose(0, 2, 1, 3).reshape(2, N, D)
        h = ln(x + y @ Wo + bov)
        gate = h @ wgm.T
        up = h @ wum.T
        gated = gate / (1 + np.exp(-gate)) * up
        return ln(h + gated @ wdm.T)

    got = kernel(L0_attn_x=x, L0_attn_Wq=Wq, L0_attn_Wo=Wo, L0_attn_bo=bov,
                 L0_ffn_w_gate=wgm, L0_ffn_w_up=wum, L0_ffn_w_down=wdm)
    want = ref()
    err = np.sqrt(np.mean((got - want) ** 2) / np.mean(want ** 2))
    print("rel l2:", err)
    print("max abs:", np.abs(got - want).max())


# version 08
# speedup vs baseline: 1.1563x; 1.0305x over previous
import numpy as np
import concourse.bass as bass
import concourse.mybir as mybir
from concourse.bass_utils import run_bass_kernel_spmd

F32 = mybir.dt.float32
BF16 = mybir.dt.bfloat16
AF = mybir.ActivationFunctionType
ALU = mybir.AluOpType
AX = mybir.AxisListType

N = 4096
D = 2048
HD = 128
NHC = 4
MINE = 1024
HID = 8192
NDC = D // 128
NHT = HID // 128
NKT = N // 128
NTT = MINE // 128
ISCALE = float(1.0 / np.sqrt(HD))
EPS = 1e-5
GROUPS = [[0, 1, 2, 3], [4, 5, 6, 7]]


def build():
    nc = bass.Bass("TRN2", target_bir_lowering=False)

    xt = nc.dram_tensor("xt", [D, N], F32, kind="ExternalInput").ap()
    xres = nc.dram_tensor("xres", [MINE, D], F32, kind="ExternalInput").ap()
    wq = nc.dram_tensor("wq", [128, NDC, NHC, HD], F32, kind="ExternalInput").ap()
    wo = nc.dram_tensor("wo", [D, D], F32, kind="ExternalInput").ap()
    bo = nc.dram_tensor("bo", [D], F32, kind="ExternalInput").ap()
    wgu = nc.dram_tensor("wgu", [NHT, 128, 2, NDC, 128], F32, kind="ExternalInput").ap()
    wd = nc.dram_tensor("wd", [HID, D], F32, kind="ExternalInput").ap()
    eye = nc.dram_tensor("eye", [128, 128], F32, kind="ExternalInput").ap()
    out = nc.dram_tensor("out", [MINE, D], F32, kind="ExternalOutput").ap()

    ag_send = [nc.dram_tensor(f"ag_send{h}", [HD, N], BF16) for h in range(NHC)]
    ag_recv = [nc.dram_tensor(f"ag_recv{h}", [4 * HD, N], BF16) for h in range(NHC)]
    zdr = nc.dram_tensor("zdr", [NHC, N], F32)
    h1d = nc.dram_tensor("h1d", [MINE, D], F32)
    gtd = nc.dram_tensor("gtd", [HID, MINE], BF16)
    wdb = nc.dram_tensor("wdb", [HID, D], BF16)

    sem_names = [
        "s_pre", "s_xt0", "s_xt1", "s_xt2", "s_xt3",
        "s_qev", "s_peq", "s_vt", "s_vte",
        "s_sS", "s_exp", "s_zac", "s_av", "s_otev", "s_zrow", "s_zw", "s_zbc", "s_otA", "s_otn",
        "s_ag", "s_cc",
        "s_wo", "s_otg0", "s_otg1", "s_otg2", "s_otg3", "s_xr0", "s_xr1", "s_bo", "s_wope", "s_ypsfree",
        "s_var1", "s_actr", "s_h1f", "s_h1d0", "s_h1d1", "s_tp", "s_tpe",
        "s_wg0", "s_wg1", "s_wg2", "s_gpe", "s_upe", "s_silu", "s_gt",
        "s_gtw0", "s_gtw1", "s_gtw2",
        "s_wdr0", "s_wdr1", "s_wdr2", "s_wdr3",
        "s_wdw0", "s_wdw1", "s_wdw2", "s_wdw3",
        "s_gtr0", "s_gtr1", "s_wdd0", "s_wdd1", "s_wdd2", "s_wdd3", "s_dhc", "s_y2free",
        "s_var2", "s_actr2", "s_h1ld0", "s_h1ld1", "s_ovr", "s_out0", "s_out1",
    ]

    import contextlib
    with contextlib.ExitStack() as est:
        S = {n: est.enter_context(nc.semaphore(n)) for n in sem_names}
        ps = est.enter_context(nc.psum_tensor("ps", [128, 4096], F32))

        def bank(i, n=1):
            return ps[:, 512 * i: 512 * (i + n)]

        def bank_bf(i):
            return ps[:, 512 * i: 512 * i + 64].bitcast(BF16)

        def bank32(i):
            return ps[:, 512 * i: 512 * i + 128]

        eye_bf = est.enter_context(nc.sbuf_tensor("eye_bf", [128, 128], BF16))
        eye32 = est.enter_context(nc.sbuf_tensor("eye32", [128, 128], F32))
        big64 = est.enter_context(nc.sbuf_tensor("big64", [128, 2 * NHT * 256], BF16))
        wo_view = big64[:, :].rearrange("p (dc n) -> p dc n", dc=NDC)
        gtp_view = big64[:, :].rearrange("p (b hc m) -> p b hc m", b=2, hc=NHT)
        rs = est.enter_context(nc.sbuf_tensor("rs", [128, 1], F32))
        negmean = est.enter_context(nc.sbuf_tensor("negmean", [128, 1], F32))
        ssq = est.enter_context(nc.sbuf_tensor("ssq", [128, 1], F32))
        var_ = est.enter_context(nc.sbuf_tensor("var_", [128, 1], F32))
        lnv = est.enter_context(nc.sbuf_tensor("lnv", [128, 1], F32))
        rstd = est.enter_context(nc.sbuf_tensor("rstd", [128, 1], F32))
        epsv = est.enter_context(nc.sbuf_tensor("epsv", [128, 1], F32))

        regs = {}

        with (
            nc.sbuf_tensor("qT", [128, NHC, N], BF16) as qT,
            nc.sbuf_tensor("v_sb", [128, NHC, NKT, HD], BF16) as v_sb,
            nc.sbuf_tensor("xt_sb", [128, 2, 2, 1024], BF16) as xt_sb,
            nc.sbuf_tensor("wq_sb", [128, NDC, NHC, HD], BF16) as wq_sb,
            nc.sbuf_tensor("P_sb", [128, 2, 1024], BF16) as P_sb,
            nc.sbuf_tensor("ot_raw", [128, 2, N], F32) as ot_raw,
            nc.sbuf_tensor("zp", [128, NHC, 4, NKT], F32) as zp,
            nc.sbuf_tensor("zs", [128, NKT], F32) as zs,
            nc.sbuf_tensor("zinv_s", [128, NKT], F32) as zinv_s,
            nc.sbuf_tensor("zrowt", [32, 128], F32) as zrowt,
            nc.sbuf_tensor("zinvb", [128, N // 2], F32) as zinvb,
            nc.sbuf_tensor("otn", [128, N // 2], BF16) as otn,
            nc.Block() as blk,
        ):
            @blk.gpsimd
            def _(g):
                g.dma_start(out=wq_sb[:, :, :, :].rearrange("p dc h c -> p (dc h c)"),
                            in_=wq.rearrange("p dc h c -> p (dc h c)")).then_inc(S["s_pre"], 16)
                g.dma_start(out=eye_bf[:, :], in_=eye).then_inc(S["s_pre"], 16)
                xt_r = xt.rearrange("(dc2 two p) t -> dc2 p two t", two=2, p=128)
                for s4 in range(4):
                    for dc2 in range(NDC // 2):
                        j2 = 8 * s4 + dc2
                        if j2 >= 2:
                            g.wait_ge(S["s_peq"], 2 * (j2 - 1))
                        g.dma_start(
                            out=xt_sb[:, j2 % 2, :, :],
                            in_=xt_r[dc2, :, :, s4 * 1024:(s4 + 1) * 1024],
                        ).then_inc(S[f"s_xt{j2 % 2}"], 16)
                for dq in range(4):
                    g.dma_start(
                        out=wo_view[:, dq * 4:(dq + 1) * 4, :],
                        in_=wo[dq * 512:(dq + 1) * 512, :].rearrange("(dc p) n -> p dc n", p=128),
                    ).then_inc(S["s_wo"], 16)
                for h in range(NHC):
                    g.wait_ge(S["s_otA"], h + 1)
                    g.dma_start(out=ag_send[h].ap()[:, 0:N // 2], in_=otn[:, :]).then_inc(S["s_ag"], 16)
                    g.wait_ge(S["s_otn"], h + 1)
                    g.dma_start(out=ag_send[h].ap()[:, N // 2:N], in_=otn[:, :]).then_inc(S["s_ag"], 16)
                    g.wait_ge(S["s_ag"], 32 * (h + 1))
                    g.collective_compute(
                        "AllGather", ALU.bypass, replica_groups=GROUPS,
                        ins=[ag_send[h].ap()], outs=[ag_recv[h].ap()],
                    ).then_inc(S["s_cc"], 1)


            @blk.tensor
            def _(t):
                t.wait_ge(S["s_pre"], 32)
                for s4 in range(4):
                    for dc in range(NDC):
                        j2 = 8 * s4 + dc // 2
                        if dc % 2 == 0:
                            t.wait_ge(S[f"s_xt{j2 % 2}"], 16 * (j2 // 2 + 1))
                        if dc == 0 and s4 > 0:
                            t.wait_ge(S["s_qev"], 4 * s4)
                        ins = None
                        for h in range(NHC):
                            for half in range(2):
                                ins = t.matmul(
                                    bank(2 * h + half),
                                    wq_sb[:, dc, h, :],
                                    xt_sb[:, j2 % 2, dc % 2, 512 * half:512 * (half + 1)],
                                    start=(dc == 0), stop=(dc == NDC - 1),
                                )
                        ins.then_inc(S["s_peq"], 1)
                for h in range(NHC):
                    for kt in range(NKT):
                        j = NKT * h + kt
                        if j == 0:
                            t.wait_ge(S["s_qev"], 16)
                        if j >= 2:
                            t.wait_ge(S["s_vte"], j - 1)
                        t.transpose(
                            bank_bf(j % 2),
                            qT[:, h, kt * 128:(kt + 1) * 128],
                            eye_bf[:, :],
                        ).then_inc(S["s_vt"], 1)
                for gidx in range(16):
                    h, qq = gidx // 4, gidx % 4
                    po = gidx % 2

                    def emit_S(kt, gidx=gidx, h=h, qq=qq):
                        idx = 32 * gidx + kt
                        if idx == 0:
                            t.wait_ge(S["s_vte"], 128)
                        par = kt % 2
                        ins = None
                        for half in range(2):
                            ins = t.matmul(
                                bank(2 * par + half),
                                qT[:, h, kt * 128:(kt + 1) * 128],
                                qT[:, h, qq * 1024 + 512 * half: qq * 1024 + 512 * (half + 1)],
                                start=True, stop=True,
                            )
                        ins.then_inc(S["s_sS"], 1)

                    def emit_AV(kt, gidx=gidx, h=h, po=po):
                        idx = 32 * gidx + kt
                        t.wait_ge(S["s_exp"], idx + 1)
                        if kt == 0 and gidx >= 2:
                            t.wait_ge(S["s_otev"], gidx - 1)
                        ins = None
                        for half in range(2):
                            ins = t.matmul(
                                bank(4 + 2 * po + half),
                                v_sb[:, h, kt, :],
                                P_sb[:, kt % 2, 512 * half:512 * (half + 1)],
                                start=(kt == 0), stop=(kt == NKT - 1),
                            )
                        ins.then_inc(S["s_av"], 1)

                    emit_S(0)
                    emit_S(1)
                    for kt in range(NKT):
                        emit_AV(kt)
                        if kt + 2 < NKT:
                            emit_S(kt + 2)

            @blk.scalar
            def _(sc):
                for gidx in range(16):
                    h, qq = gidx // 4, gidx % 4
                    for kt in range(NKT):
                        idx = 32 * gidx + kt
                        par = kt % 2
                        sc.wait_ge(S["s_sS"], idx + 1)
                        if idx >= 2:
                            sc.wait_ge(S["s_av"], idx - 1)
                        sc.activation(
                            P_sb[:, par, :],
                            ps[:, par * 1024:(par + 1) * 1024],
                            AF.Exp, scale=ISCALE,
                            accum_out=zp[:, h, qq, kt:kt + 1].opt(),
                        ).then_inc(S["s_exp"], 1)

            @blk.vector
            def _(v):
                v.memset(epsv[:, :], EPS)
                for s4 in range(4):
                    v.wait_ge(S["s_peq"], 16 * (s4 + 1))
                    for h in range(NHC):
                        v.tensor_copy(
                            qT[:, h, s4 * 1024:(s4 + 1) * 1024], bank(2 * h, 2)
                        ).then_inc(S["s_qev"], 1)
                for h in range(NHC):
                    for kt in range(NKT):
                        j = NKT * h + kt
                        v.wait_ge(S["s_vt"], j + 1)
                        v.tensor_copy(v_sb[:, h, kt, :], bank_bf(j % 2)).then_inc(S["s_vte"], 1)

                def emit_otn(hp):
                    v.wait_ge(S["s_zbc"], 32 * hp + 16)
                    if hp >= 1:
                        v.wait_ge(S["s_ag"], 32 * hp)
                    ins = None
                    for q2 in range(4):
                        if q2 == 2:
                            ins.then_inc(S["s_otA"], 1)
                            v.wait_ge(S["s_zbc"], 32 * hp + 32)
                            v.wait_ge(S["s_ag"], 32 * hp + 16)
                        elif q2 > 0:
                            v.drain()
                        ins = v.tensor_mul(
                            otn[:, (q2 % 2) * 1024:(q2 % 2 + 1) * 1024],
                            ot_raw[:, hp % 2, q2 * 1024:(q2 + 1) * 1024],
                            zinvb[:, (q2 % 2) * 1024:(q2 % 2 + 1) * 1024],
                        )
                    ins.then_inc(S["s_otn"], 1)

                for gidx in range(16):
                    h, qq = gidx // 4, gidx % 4
                    po = gidx % 2
                    v.wait_ge(S["s_av"], 32 * (gidx + 1))
                    v.tensor_copy(
                        ot_raw[:, h % 2, qq * 1024:(qq + 1) * 1024], bank(4 + 2 * po, 2)
                    ).then_inc(S["s_otev"], 1)
                    if qq == 3:
                        v.tensor_add(zs[:, :], zp[:, h, 0, :], zp[:, h, 1, :])
                        v.drain()
                        v.tensor_add(zs[:, :], zs[:, :], zp[:, h, 2, :])
                        v.drain()
                        v.tensor_add(zs[:, :], zs[:, :], zp[:, h, 3, :])
                        v.drain()
                        v.reciprocal(zinv_s[:, :], zs[:, :])
                        v.drain()
                        ins = None
                        for pb in range(4):
                            if pb > 0:
                                v.drain()
                            ins = v.transpose(
                                zrowt[0:32, 32 * pb:32 * (pb + 1)],
                                zinv_s[32 * pb:32 * (pb + 1), 0:32],
                            )
                        ins.then_inc(S["s_zrow"], 1)
                    if qq == 0 and gidx >= 4:
                        emit_otn(gidx // 4 - 1)
                emit_otn(3)

            @blk.sync
            def _(sy):
                pid = sy.partition_id()
                regs["r4"] = sy.compute_val(pid % 4)
                sy.dma_start(out=eye32[:, :], in_=eye).then_inc(S["s_bo"], 16)
                for h in range(NHC):
                    sy.wait_ge(S["s_zrow"], h + 1)
                    sy.dma_start(
                        out=zdr.ap()[h].rearrange("(kt p) -> kt p", p=128),
                        in_=zrowt[0:32, :],
                    ).then_inc(S["s_zw"], 16)
                    sy.wait_ge(S["s_zw"], 16 * (h + 1))
                    if h >= 1:
                        sy.wait_ge(S["s_otn"], h)
                    sy.dma_start(
                        out=zinvb[:, :],
                        in_=zdr.ap()[h][0:N // 2].partition_broadcast(128),
                    ).then_inc(S["s_zbc"], 16)
                    sy.wait_ge(S["s_otA"], h + 1)
                    sy.dma_start(
                        out=zinvb[:, :],
                        in_=zdr.ap()[h][N // 2:N].partition_broadcast(128),
                    ).then_inc(S["s_zbc"], 16)

        with (
            nc.sbuf_tensor("wgu_sb", [128, 3, 2, NDC, 128], BF16) as wgu_sb,
            nc.sbuf_tensor("h1T", [128, NDC, MINE], BF16) as h1T,
        ):
            with (
                nc.sbuf_tensor("otg", [128, NDC, MINE], BF16) as otg,
                nc.sbuf_tensor("xres_sb", [128, 2, D], F32) as xres_sb,
                nc.sbuf_tensor("bo_bc", [128, D], F32) as bo_bc,
                nc.sbuf_tensor("z1", [128, D], F32) as z1,
                nc.sbuf_tensor("h1f", [128, 2, D], F32) as h1f,
                nc.Block() as b2,
            ):
                @b2.sync
                def _(sy):
                    def otg_load(hl):
                        for r in range(4):
                            sy.dma_start(
                                out=otg[:, 4 * r + hl, :],
                                in_=ag_recv[hl].ap()[r * 128:(r + 1) * 128, bass.ts(regs["r4"], MINE)],
                            ).then_inc(S[f"s_otg{hl}"], 16)
                    for hl in range(3):
                        sy.wait_ge(S["s_cc"], hl + 1)
                        otg_load(hl)
                    sy.dma_start(out=bo_bc[:, :], in_=bo.partition_broadcast(128)).then_inc(S["s_bo"], 16)
                    sy.wait_ge(S["s_cc"], NHC)
                    otg_load(3)
                    for tt in range(NTT):
                        if tt >= 2:
                            sy.wait_ge(S["s_h1f"], tt - 1)
                        sy.dma_start(
                            out=xres_sb[:, tt % 2, :],
                            in_=xres[tt * 128:(tt + 1) * 128, :],
                        ).then_inc(S[f"s_xr{tt % 2}"], 16)

                @b2.tensor
                def _(t):
                    t.wait_ge(S["s_exp"], 512)
                    t.wait_ge(S["s_otev"], 16)

                    def emit_tr(tt_src):
                        t.wait_ge(S["s_h1f"], tt_src + 1)
                        for j16 in range(16):
                            jj = 16 * tt_src + j16
                            if jj == 0:
                                pass
                            if jj >= 2:
                                t.wait_ge(S["s_tpe"], jj - 1)
                            t.transpose(
                                bank32(4 + jj % 2),
                                h1f[:, tt_src % 2, j16 * 128:(j16 + 1) * 128],
                                eye32[:, :],
                            ).then_inc(S["s_tp"], 1)

                    dc_order = [d for d in range(NDC) if d % 4 != 3] + [d for d in range(NDC) if d % 4 == 3]
                    for tt in range(NTT):
                        if tt == 0:
                            t.wait_ge(S["s_wo"], 16 * 4)
                        if tt >= 1:
                            t.wait_ge(S["s_ypsfree"], tt)
                        ins = None
                        for di, dc in enumerate(dc_order):
                            if tt == 0:
                                t.wait_ge(S[f"s_otg{dc % 4}"], 16 * 4)
                            for half in range(4):
                                ins = t.matmul(
                                    bank(half),
                                    otg[:, dc, tt * 128:(tt + 1) * 128],
                                    wo_view[:, dc, 512 * half:512 * (half + 1)],
                                    start=(di == 0), stop=(di == NDC - 1),
                                )
                        ins.then_inc(S["s_wope"], 1)
                        if tt >= 1:
                            emit_tr(tt - 1)
                    emit_tr(NTT - 1)

                @b2.vector
                def _(v):
                    def emit_trev(tt_src):
                        for j16 in range(16):
                            jj = 16 * tt_src + j16
                            v.wait_ge(S["s_tp"], jj + 1)
                            v.tensor_copy(
                                h1T[:, j16, tt_src * 128:(tt_src + 1) * 128],
                                bank32(4 + jj % 2),
                            ).then_inc(S["s_tpe"], 1)

                    for tt in range(NTT):
                        v.wait_ge(S["s_wope"], tt + 1)
                        if tt == 0:
                            v.wait_ge(S["s_bo"], 32)
                        v.scalar_tensor_tensor(
                            z1[:, :], ps[:, 0:2048], 1.0, bo_bc[:, :], ALU.mult, ALU.add
                        ).then_inc(S["s_ypsfree"], 1)
                        v.drain()
                        v.wait_ge(S[f"s_xr{tt % 2}"], 16 * (tt // 2 + 1))
                        v.tensor_add(z1[:, :], z1[:, :], xres_sb[:, tt % 2, :])
                        v.drain()
                        v.reduce_sum(rs[:, :], z1[:, :], axis=AX.X)
                        v.drain()
                        v.tensor_scalar_mul(negmean[:, :], rs[:, :], -1.0 / D)
                        v.drain()
                        v.tensor_scalar_add(z1[:, :], z1[:, :], negmean[:, :])
                        v.drain()
                        if tt >= 2:
                            v.wait_ge(S[f"s_h1d{tt % 2}"], 16 * (tt // 2))
                            v.wait_ge(S["s_tp"], 16 * (tt - 1))
                        v.scalar_tensor_tensor(
                            h1f[:, tt % 2, :], z1[:, :], 1.0, z1[:, :], ALU.mult, ALU.mult,
                            accum_out=ssq[:, :],
                        )
                        v.drain()
                        v.tensor_scalar_mul(var_[:, :], ssq[:, :], 1.0 / D).then_inc(S["s_var1"], 1)
                        v.wait_ge(S["s_actr"], tt + 1)
                        v.tensor_scalar_mul(h1f[:, tt % 2, :], z1[:, :], rstd[:, :]).then_inc(S["s_h1f"], 1)
                        v.drain()
                        if tt >= 1:
                            emit_trev(tt - 1)
                    emit_trev(NTT - 1)

                @b2.scalar
                def _(sc):
                    for tt in range(NTT):
                        sc.wait_ge(S["s_var1"], tt + 1)
                        sc.activation(lnv[:, :], var_[:, :], AF.Ln, bias=epsv[:, :])
                        sc.drain()
                        sc.activation(rstd[:, :], lnv[:, :], AF.Exp, scale=-0.5).then_inc(S["s_actr"], 1)
                        sc.drain()
                        sc.wait_ge(S["s_h1f"], tt + 1)
                        sc.dma_start(
                            out=h1d.ap()[tt * 128:(tt + 1) * 128, :],
                            in_=h1f[:, tt % 2, :],
                        ).then_inc(S[f"s_h1d{tt % 2}"], 16)

            with (
                nc.sbuf_tensor("sg", [128, 2, 1024], F32) as sg,
                nc.sbuf_tensor("wdstage", [128, 2, D], BF16) as wdstage,
                nc.sbuf_tensor("gtw", [128, 3, 1024], BF16) as gtw,
                nc.sbuf_tensor("wdd", [128, 4, D], BF16) as wdd,
                nc.sbuf_tensor("h1t_sb", [128, 2, D], F32) as h1t_sb,
                nc.sbuf_tensor("outst", [128, 2, D], F32) as outst,
                nc.sbuf_tensor("z2", [128, 2, D], F32) as z2,
                nc.Block() as b3,
            ):
                @b3.gpsimd
                def _(g):
                    def load_wgu(ht):
                        g.dma_start(
                            out=wgu_sb[:, ht % 3, :, :, :].rearrange("p k dc c -> p (k dc c)"),
                            in_=wgu[ht].rearrange("p k dc c -> p (k dc c)"),
                        ).then_inc(S[f"s_wg{ht % 3}"], 16)

                    load_wgu(0)
                    load_wgu(1)
                    load_wgu(2)
                    g.wait_ge(S["s_tpe"], 128)
                    g.wait_ge(S["s_h1d0"], 16 * (NTT // 2))
                    g.wait_ge(S["s_h1d1"], 16 * (NTT // 2))
                    for ht in range(NHT):
                        if ht + 3 < NHT:
                            g.wait_ge(S["s_upe"], ht + 1)
                            load_wgu(ht + 3)
                        if ht >= 2:
                            g.wait_ge(S[f"s_wdw{ht % 2}"], 16 * (ht // 2))
                        g.dma_start(
                            out=wdstage[:, ht % 2, :],
                            in_=wd[ht * 128:(ht + 1) * 128, :],
                        ).then_inc(S[f"s_wdr{ht % 2}"], 16)
                        hcc = ht - 1
                        if hcc >= 0:
                            g.wait_ge(S[f"s_wdr{hcc % 2}"], 16 * (hcc // 2 + 1))
                            g.dma_start(
                                out=wdb.ap()[hcc * 128:(hcc + 1) * 128, :],
                                in_=wdstage[:, hcc % 2, :],
                            ).then_inc(S[f"s_wdw{hcc % 2}"], 16)
                    hcc = NHT - 1
                    g.wait_ge(S[f"s_wdr{hcc % 2}"], 16 * (hcc // 2 + 1))
                    g.dma_start(
                        out=wdb.ap()[hcc * 128:(hcc + 1) * 128, :],
                        in_=wdstage[:, hcc % 2, :],
                    ).then_inc(S[f"s_wdw{hcc % 2}"], 16)

                @b3.tensor
                def _(t):
                    for ht in range(NHT):
                        t.wait_ge(S[f"s_wg{ht % 3}"], 16 * (ht // 3 + 1))
                        if ht == 0:
                            t.wait_ge(S["s_tpe"], 128)
                        if ht >= 2:
                            t.wait_ge(S["s_silu"], ht - 1)
                        ins = None
                        for dc in range(NDC):
                            for half in range(2):
                                ins = t.matmul(
                                    bank(2 * (ht % 2) + half),
                                    wgu_sb[:, ht % 3, 0, dc, :],
                                    h1T[:, dc, 512 * half:512 * (half + 1)],
                                    start=(dc == 0), stop=(dc == NDC - 1),
                                )
                        ins.then_inc(S["s_gpe"], 1)
                        if ht >= 2:
                            t.wait_ge(S["s_gt"], ht - 1)
                        ins = None
                        for dc in range(NDC):
                            for half in range(2):
                                ins = t.matmul(
                                    bank(4 + 2 * (ht % 2) + half),
                                    wgu_sb[:, ht % 3, 1, dc, :],
                                    h1T[:, dc, 512 * half:512 * (half + 1)],
                                    start=(dc == 0), stop=(dc == NDC - 1),
                                )
                        ins.then_inc(S["s_upe"], 1)
                    t.wait_ge(S["s_silu"], NHT)
                    t.wait_ge(S["s_gt"], NHT)
                    for p4 in range(4):
                        t.wait_ge(S[f"s_gtr{p4 % 2}"], 128 * (p4 // 2 + 1))
                        if p4 >= 1:
                            t.wait_ge(S["s_y2free"], 2 * p4)
                        for hc in range(NHT):
                            gidx2 = 64 * p4 + hc
                            t.wait_ge(S[f"s_wdd{gidx2 % 4}"], 16 * (gidx2 // 4 + 1))
                            ins = None
                            for side in range(2):
                                for half in range(4):
                                    ins = t.matmul(
                                        bank(4 * side + half),
                                        gtp_view[:, p4 % 2, hc, 128 * side:128 * (side + 1)],
                                        wdd[:, gidx2 % 4, 512 * half:512 * (half + 1)],
                                        start=(hc == 0), stop=(hc == NHT - 1),
                                    )
                            ins.then_inc(S["s_dhc"], 1)

                @b3.scalar
                def _(sc):
                    for ht in range(NHT):
                        sc.wait_ge(S["s_gpe"], ht + 1)
                        if ht >= 2:
                            sc.wait_ge(S["s_gt"], ht - 1)
                        sc.activation(
                            sg[:, ht % 2, :],
                            ps[:, (ht % 2) * 1024:(ht % 2 + 1) * 1024],
                            AF.Silu,
                        ).then_inc(S["s_silu"], 1)
                    sc.wait_ge(S["s_gtw0"], 16 * 22)
                    sc.wait_ge(S["s_gtw1"], 16 * 21)
                    sc.wait_ge(S["s_gtw2"], 16 * 21)
                    sc.wait_ge(S["s_wdw0"], 16 * (NHT // 2))
                    sc.wait_ge(S["s_wdw1"], 16 * (NHT // 2))
                    sc.wait_ge(S["s_h1d0"], 16 * (NTT // 2))
                    sc.wait_ge(S["s_h1d1"], 16 * (NTT // 2))
                    def ln2_act(pp):
                        for tt in (2 * pp, 2 * pp + 1):
                            sc.wait_ge(S["s_var2"], tt + 1)
                            sc.activation(lnv[:, :], var_[:, :], AF.Ln, bias=epsv[:, :])
                            sc.drain()
                            sc.activation(rstd[:, :], lnv[:, :], AF.Exp, scale=-0.5).then_inc(S["s_actr2"], 1)
                            sc.drain()
                            sc.wait_ge(S["s_ovr"], tt + 1)
                            sc.dma_start(
                                out=out[tt * 128:(tt + 1) * 128, :],
                                in_=outst[:, tt % 2, :],
                            ).then_inc(S[f"s_out{tt % 2}"], 16)

                    for p4 in range(4):
                        for hc in range(NHT):
                            gidx2 = 64 * p4 + hc
                            if gidx2 >= 4:
                                sc.wait_ge(S["s_dhc"], gidx2 - 3)
                            sc.dma_start(
                                out=wdd[:, gidx2 % 4, :],
                                in_=wdb.ap()[hc * 128:(hc + 1) * 128, :],
                            ).then_inc(S[f"s_wdd{gidx2 % 4}"], 16)
                        for tt in (2 * p4, 2 * p4 + 1):
                            if tt >= 2:
                                sc.wait_ge(S["s_y2free"], tt - 1)
                            sc.dma_start(
                                out=h1t_sb[:, tt % 2, :],
                                in_=h1d.ap()[tt * 128:(tt + 1) * 128, :],
                            ).then_inc(S[f"s_h1ld{tt % 2}"], 16)
                        if p4 >= 1:
                            ln2_act(p4 - 1)
                    ln2_act(3)

                @b3.vector
                def _(v):
                    for ht in range(NHT):
                        v.wait_ge(S["s_silu"], ht + 1)
                        v.wait_ge(S["s_upe"], ht + 1)
                        if ht >= 3:
                            v.wait_ge(S[f"s_gtw{ht % 3}"], 16 * (ht // 3))
                        v.tensor_mul(
                            gtw[:, ht % 3, :],
                            sg[:, ht % 2, :],
                            ps[:, (4 + 2 * (ht % 2)) * 512:(6 + 2 * (ht % 2)) * 512],
                        ).then_inc(S["s_gt"], 1)
                    for p4 in range(4):
                        v.wait_ge(S["s_dhc"], 64 * (p4 + 1))
                        for tt in (2 * p4, 2 * p4 + 1):
                            v.wait_ge(S[f"s_h1ld{tt % 2}"], 16 * (tt // 2 + 1))
                            v.scalar_tensor_tensor(
                                z2[:, tt % 2, :], ps[:, (tt % 2) * 2048:(tt % 2 + 1) * 2048], 1.0,
                                h1t_sb[:, tt % 2, :], ALU.mult, ALU.add,
                            ).then_inc(S["s_y2free"], 1)
                            v.drain()
                        for tt in (2 * p4, 2 * p4 + 1):
                            v.reduce_sum(rs[:, :], z2[:, tt % 2, :], axis=AX.X)
                            v.drain()
                            v.tensor_scalar_mul(negmean[:, :], rs[:, :], -1.0 / D)
                            v.drain()
                            v.tensor_scalar_add(z2[:, tt % 2, :], z2[:, tt % 2, :], negmean[:, :])
                            v.drain()
                            if tt >= 2:
                                v.wait_ge(S[f"s_out{tt % 2}"], 16 * (tt // 2))
                            v.scalar_tensor_tensor(
                                outst[:, tt % 2, :], z2[:, tt % 2, :], 1.0, z2[:, tt % 2, :],
                                ALU.mult, ALU.mult, accum_out=ssq[:, :],
                            )
                            v.drain()
                            v.tensor_scalar_mul(var_[:, :], ssq[:, :], 1.0 / D).then_inc(S["s_var2"], 1)
                            v.wait_ge(S["s_actr2"], tt + 1)
                            v.tensor_scalar_mul(outst[:, tt % 2, :], z2[:, tt % 2, :], rstd[:, :]).then_inc(S["s_ovr"], 1)
                            v.drain()

                @b3.sync
                def _(sy):
                    for ht in range(NHT):
                        sy.wait_ge(S["s_gt"], ht + 1)
                        sy.dma_start(
                            out=gtd.ap()[ht * 128:(ht + 1) * 128, :],
                            in_=gtw[:, ht % 3, :],
                        ).then_inc(S[f"s_gtw{ht % 3}"], 16)
                    sy.wait_ge(S["s_wope"], NTT)
                    sy.wait_ge(S["s_gtw0"], 16 * 22)
                    sy.wait_ge(S["s_gtw1"], 16 * 21)
                    sy.wait_ge(S["s_gtw2"], 16 * 21)
                    sy.wait_ge(S["s_wdw0"], 16 * (NHT // 2))
                    sy.wait_ge(S["s_wdw1"], 16 * (NHT // 2))
                    sy.wait_ge(S["s_h1d0"], 16 * (NTT // 2))
                    sy.wait_ge(S["s_h1d1"], 16 * (NTT // 2))
                    gtd_r = gtd.ap().rearrange("(hc k) t -> k hc t", k=128)
                    for p4 in range(4):
                        if p4 >= 2:
                            sy.wait_ge(S["s_dhc"], 64 * (p4 - 1))
                        for oc in range(8):
                            sy.dma_start(
                                out=gtp_view[:, p4 % 2, 8 * oc:8 * (oc + 1), :],
                                in_=gtd_r[:, 8 * oc:8 * (oc + 1), 256 * p4:256 * (p4 + 1)],
                            ).then_inc(S[f"s_gtr{p4 % 2}"], 16)
                    sy.wait_ge(S["s_out0"], 16 * (NTT // 2))
                    sy.wait_ge(S["s_out1"], 16 * (NTT // 2))

    return nc


_NC_CACHE = []


def kernel(L0_attn_x, L0_attn_Wq, L0_attn_Wo, L0_attn_bo,
           L0_ffn_w_gate, L0_ffn_w_up, L0_ffn_w_down):
    x = np.asarray(L0_attn_x, dtype=np.float32)
    Wq = np.asarray(L0_attn_Wq, dtype=np.float32)
    Wo = np.ascontiguousarray(np.asarray(L0_attn_Wo, dtype=np.float32))
    bov = np.ascontiguousarray(np.asarray(L0_attn_bo, dtype=np.float32))
    wgt = np.asarray(L0_ffn_w_gate, dtype=np.float32).reshape(NHT, 128, NDC, 128).transpose(0, 3, 2, 1)
    wut = np.asarray(L0_ffn_w_up, dtype=np.float32).reshape(NHT, 128, NDC, 128).transpose(0, 3, 2, 1)
    wguv = np.ascontiguousarray(np.stack([wgt, wut], axis=2))
    wdT = np.ascontiguousarray(np.asarray(L0_ffn_w_down, dtype=np.float32).T)
    eye = np.eye(128, dtype=np.float32)
    xT = [np.ascontiguousarray(x[b].T) for b in range(2)]

    in_maps = []
    for c in range(8):
        b, r4 = c // 4, c % 4
        wqt = np.ascontiguousarray(
            Wq[:, r4 * 512:(r4 + 1) * 512].reshape(NDC, 128, NHC, 128).transpose(1, 0, 2, 3))
        in_maps.append(dict(
            xt=xT[b],
            xres=np.ascontiguousarray(x[b, r4 * MINE:(r4 + 1) * MINE]),
            wq=wqt,
            wo=Wo, bo=bov, wgu=wguv, wd=wdT, eye=eye,
        ))

    if not _NC_CACHE:
        _NC_CACHE.append(build())
    nc = _NC_CACHE[0]
    res = run_bass_kernel_spmd(nc, in_maps, core_ids=list(range(8)))
    pieces = [res.results[c]["out"] for c in range(8)]
    return np.concatenate(pieces, axis=0).reshape(2, N, D)


if __name__ == "__main__":
    rng = np.random.default_rng(1)
    x = rng.standard_normal((2, N, D), dtype=np.float32)
    Wq = (rng.standard_normal((D, D), dtype=np.float32) * D ** -0.5).astype(np.float32)
    Wo = (rng.standard_normal((D, D), dtype=np.float32) * D ** -0.5).astype(np.float32)
    bov = rng.standard_normal(D).astype(np.float32) * 0.1
    wgm = (rng.uniform(-1, 1, (HID, D)) / np.sqrt(D)).astype(np.float32)
    wum = (rng.uniform(-1, 1, (HID, D)) / np.sqrt(D)).astype(np.float32)
    wdm = (rng.uniform(-1, 1, (D, HID)) / np.sqrt(HID)).astype(np.float32)

    def ref():
        def ln(a):
            m = a.mean(-1, keepdims=True)
            v = a.var(-1, keepdims=True)
            return (a - m) / np.sqrt(v + EPS)
        q = x @ Wq
        qh = q.reshape(2, N, 16, HD).transpose(0, 2, 1, 3)
        s = np.einsum("bhnd,bhmd->bhnm", qh, qh) * HD ** -0.5
        s = s - s.max(-1, keepdims=True)
        a = np.exp(s)
        a /= a.sum(-1, keepdims=True)
        y = np.einsum("bhnm,bhmd->bhnd", a, qh).transpose(0, 2, 1, 3).reshape(2, N, D)
        h = ln(x + y @ Wo + bov)
        gate = h @ wgm.T
        up = h @ wum.T
        gated = gate / (1 + np.exp(-gate)) * up
        return ln(h + gated @ wdm.T)

    got = kernel(L0_attn_x=x, L0_attn_Wq=Wq, L0_attn_Wo=Wo, L0_attn_bo=bov,
                 L0_ffn_w_gate=wgm, L0_ffn_w_up=wum, L0_ffn_w_down=wdm)
    want = ref()
    err = np.sqrt(np.mean((got - want) ** 2) / np.mean(want ** 2))
    print("rel l2:", err)
    print("max abs:", np.abs(got - want).max())
